# version 11
# baseline (speedup 1.0000x reference)
import numpy as np
import ml_dtypes
from contextlib import ExitStack

import concourse.bass as bass
import concourse.mybir as mybir
from concourse.bass_utils import run_bass_kernel_spmd

F32 = mybir.dt.float32
BF16 = mybir.dt.bfloat16
AF = mybir.ActivationFunctionType
ALU = mybir.AluOpType
AX = mybir.AxisListType.X
BF = ml_dtypes.bfloat16

ENGS = ("sync", "tensor", "vector", "scalar", "gpsimd")

D = 2048
SEQ = 4096
NHL = 8
DH = 128
INV_SQRT_DH = 0.08838834764831845
NEGBIG = -1.0e10 * INV_SQRT_DH


class Sem:
    __slots__ = ("h", "n")


class Builder:

    SELF_WAIT = ("vector", "scalar", "gpsimd")

    def __init__(self, nc, ctx):
        self.nc, self.ctx = nc, ctx
        self.q = {e: [] for e in ENGS}
        self.esem = {}
        self.scopes = [ctx]
        self.all_sems = []

    def set_engine_sems(self, **kw):
        self.esem.update(kw)

    def sem(self, name):
        s = Sem()
        s.h = self.ctx.enter_context(self.nc.semaphore(name))
        s.n = 0
        self.all_sems.append(s)
        return s

    _uid = 0

    def sb(self, name, shape, dt=F32):
        Builder._uid += 1
        return self.scopes[-1].enter_context(
            self.nc.sbuf_tensor(f"{name}_u{Builder._uid}", shape, dt))

    def ps(self, name, shape, dt=F32):
        Builder._uid += 1
        return self.scopes[-1].enter_context(
            self.nc.psum_tensor(f"{name}_u{Builder._uid}", shape, dt))

    from contextlib import contextmanager as _cm

    @_cm
    def scope(self):
        es = ExitStack()
        self.scopes.append(es)
        try:
            yield
        finally:
            es.close()
            self.scopes.pop()

    def barrier(self):
        wl = [(s.h, s.n) for s in self.all_sems]
        for eng in ENGS:
            self.q[eng].append(
                lambda e, wl=wl: [e.wait_ge(h, v) for h, v in wl if v > 0])

    def on(self, eng, fn, inc=None, waits=()):
        wl = [(s.h, s.n if v is None else v) for (s, v) in waits]
        if eng in self.SELF_WAIT and eng in self.esem:
            es = self.esem[eng]
            wl.append((es.h, es.n))
        if inc is not None:
            sem, k = inc
            sem.n += k
            ihk = (sem.h, k)
        else:
            ihk = None

        def clo(e, wl=wl, ihk=ihk, fn=fn):
            for h, v in wl:
                if v > 0:
                    e.wait_ge(h, v)
            ins = fn(e)
            if ihk is not None:
                ins.then_inc(ihk[0], ihk[1])
        self.q[eng].append(clo)

    def waitq(self, eng, waits):
        wl = [(s.h, s.n if v is None else v) for (s, v) in waits]
        self.q[eng].append(lambda e, wl=wl: [e.wait_ge(h, v) for h, v in wl if v > 0])

    def emit(self):
        with self.nc.Block() as block:
            for eng in ENGS:
                deco = getattr(block, eng)

                def make_body(eng=eng):
                    def body(e):
                        for clo in self.q[eng]:
                            clo(e)
                    return body
                deco(make_body())


def consts_host():
    tril = np.tril(np.ones((128, 128), np.float32))
    negtril = (NEGBIG * (1.0 - tril)).astype(np.float32)
    identbf = np.eye(128, dtype=np.float32).astype(BF)
    ones1_128 = np.ones((1, 128), np.float32)
    return dict(tril=tril, negtril=negtril, identbf=identbf, ones1=ones1_128)



def build_attn0():
    nc = bass.Bass("TRN2", target_bir_lowering=False)
    xb = nc.dram_tensor("xb", [SEQ, D], F32, kind="ExternalInput").ap()
    wq = nc.dram_tensor("wq", [NHL, DH, D], F32, kind="ExternalInput").ap()
    wk = nc.dram_tensor("wk", [NHL, DH, D], F32, kind="ExternalInput").ap()
    wv = nc.dram_tensor("wv", [NHL, DH, D], F32, kind="ExternalInput").ap()
    tril = nc.dram_tensor("tril", [128, 128], F32, kind="ExternalInput").ap()
    negtril = nc.dram_tensor("negtril", [128, 128], F32, kind="ExternalInput").ap()
    identbf = nc.dram_tensor("identbf", [128, 128], BF16, kind="ExternalInput").ap()
    zT = nc.dram_tensor("zT", [NHL * DH, SEQ], BF16, kind="ExternalOutput").ap()
    xnT_d = nc.dram_tensor("xnT_d", [D, SEQ], BF16, kind="ExternalOutput").ap()

    with ExitStack() as ctx:
        B = Builder(nc, ctx)
        LD_C = B.sem("LD_C")
        LD_X = [B.sem(f"LD_X{i}") for i in range(2)]
        LD_W = B.sem("LD_W")
        LD_XNB = [B.sem(f"LD_XNB{i}") for i in range(2)]
        ST_XNT = [B.sem(f"ST_XNT{i}") for i in range(2)]
        ST_Z = B.sem("ST_Z")
        TEN = B.sem("TEN")
        VEC = B.sem("VEC")
        SCA = B.sem("SCA")
        B.set_engine_sems(vector=VEC, scalar=SCA)

        xt = [B.sb(f"xt{i}", [128, D]) for i in range(2)]
        xnbf = [B.sb(f"xnbf{i}", [128, D], BF16) for i in range(2)]
        stg = [B.sb(f"stg{i}", [128, D], BF16) for i in range(2)]
        red = B.sb("red", [128, 8])
        epsT = B.sb("epsT", [128, 1])
        trilS = B.sb("trilS", [128, 128])
        ngtS = B.sb("ngtS", [128, 128])
        idbS = B.sb("idbS", [128, 128], BF16)
        wbf = B.sb("wbf", [128, D], BF16)
        wT = {n: B.sb(f"wT_{n}", [128, D], BF16) for n in "qkv"}
        xnb = [B.sb(f"xnb{i}", [128, 16 * 512], BF16) for i in range(2)]
        QT = B.sb("QT", [128, SEQ], BF16)
        KT = B.sb("KT", [128, SEQ], BF16)
        VT = B.sb("VT", [128, SEQ], BF16)
        Vtm = B.sb("Vtm", [128, SEQ], BF16)
        S_sb = [B.sb(f"S_sb{i}", [128, SEQ]) for i in range(2)]
        if SEQ >= D:
            sqs = S_sb[0][:, 0:D]
            w32 = S_sb[1][:, 0:D]
        else:
            sqs = B.sb("sqs", [128, D])[:]
            w32 = B.sb("w32", [128, D])[:]
        P_sb = [B.sb(f"P_sb{i}", [128, SEQ], BF16) for i in range(2)]
        PT_sb = [B.sb(f"PT_sb{i}", [128, SEQ], BF16) for i in range(2)]
        mx = [B.sb(f"mx{i}", [128, 2]) for i in range(2)]
        sump = [B.sb(f"sump{i}", [128, 2]) for i in range(2)]
        zT_sb = B.sb("zT_sb", [128, SEQ], BF16)

        PS = [B.ps(f"PS{i}", [128, 512]) for i in range(6)]
        PB = [B.ps(f"PB{i}", [128, 512], BF16) for i in range(2)]

        B.on("vector", lambda e: e.memset(epsT[:], 1e-5), inc=(VEC, 1))
        B.on("sync", lambda e: e.dma_start(out=trilS[:], in_=tril), inc=(LD_C, 16))
        B.on("sync", lambda e: e.dma_start(out=ngtS[:], in_=negtril), inc=(LD_C, 16))
        B.on("sync", lambda e: e.dma_start(out=idbS[:], in_=identbf), inc=(LD_C, 16))

        xn_free = [0, 0]
        stg_free = [0, 0]
        xt_free = [0, 0]
        xt_free_s = [0, 0]
        for tt in range(SEQ // 128):
            p = tt % 2
            B.on("sync", lambda e, p=p, tt=tt: e.dma_start(
                out=xt[p][:], in_=xb[tt * 128:(tt + 1) * 128, :]),
                inc=(LD_X[p], 16), waits=[(VEC, xt_free[p]), (SCA, xt_free_s[p])])
            ld_here = LD_X[p].n
            B.on("vector", lambda e, p=p: e.reduce_sum(red[:, 0:1], xt[p][:], AX),
                 inc=(VEC, 1), waits=[(LD_X[p], ld_here)])
            B.on("scalar", lambda e, p=p: e.activation(
                sqs, xt[p][:], AF.Square, accum_out=red[:, 2:3]),
                inc=(SCA, 1), waits=[(LD_X[p], ld_here), (VEC, None)])
            B.on("vector", lambda e: e.tensor_scalar(
                red[:, 1:2], red[:, 0:1], 1.0 / D, None, ALU.mult), inc=(VEC, 1))
            B.on("vector", lambda e: e.tensor_mul(red[:, 3:4], red[:, 1:2], red[:, 1:2]),
                 inc=(VEC, 1))
            B.on("vector", lambda e: e.tensor_scalar(
                red[:, 4:5], red[:, 2:3], 1.0 / D, None, ALU.mult),
                inc=(VEC, 1), waits=[(SCA, None)])
            B.on("vector", lambda e: e.tensor_sub(red[:, 4:5], red[:, 4:5], red[:, 3:4]),
                 inc=(VEC, 1))
            B.on("scalar", lambda e: e.activation(red[:, 5:6], red[:, 4:5], AF.Ln, bias=epsT[:]),
                 inc=(SCA, 1), waits=[(VEC, None)])
            B.on("scalar", lambda e: e.activation(red[:, 6:7], red[:, 5:6], AF.Exp, scale=-0.5),
                 inc=(SCA, 1))
            B.on("vector", lambda e, p=p: e.tensor_scalar(
                xnbf[p][:], xt[p][:], red[:, 1:2], red[:, 6:7], ALU.subtract, ALU.mult),
                inc=(VEC, 1), waits=[(SCA, None), (TEN, xn_free[p])])
            xt_free[p] = VEC.n
            xt_free_s[p] = SCA.n
            for g in range(4):
                gp = g % 2
                for j in range(4):
                    dc = g * 4 + j
                    B.on("tensor", lambda e, p=p, gp=gp, j=j, dc=dc: e.transpose(
                        PB[gp][:, j * 128:(j + 1) * 128], xnbf[p][:, dc * 128:(dc + 1) * 128],
                        idbS[:]),
                        inc=(TEN, 1),
                        waits=[(VEC, None), (LD_C, 48)] if j == 0 else [])
                B.on("vector", lambda e, g=g, gp=gp, p=p: e.tensor_copy(
                    stg[p][:, g * 512:(g + 1) * 512], PB[gp][:]),
                    inc=(VEC, 1),
                    waits=([(TEN, None), (ST_XNT[p], stg_free[p])] if g == 0
                           else [(TEN, None)]))
            dst = xnT_d[:, tt * 128:(tt + 1) * 128].rearrange(
                "(dc r) c -> r dc c", r=128)
            B.on("gpsimd", lambda e, p=p, dst=dst: e.dma_start(
                out=dst, in_=stg[p][:].rearrange("r (dc c) -> r dc c", c=128)),
                inc=(ST_XNT[p], 16), waits=[(VEC, None)])
            stg_free[p] = ST_XNT[p].n
            xn_free[p] = TEN.n

        nqt = SEQ // 128
        for h in range(NHL):
            for wi, (nm, src) in enumerate((("q", wq), ("k", wk), ("v", wv))):
                B.on("sync", lambda e, src=src, h=h: e.dma_start(
                    out=w32, in_=src[h]), inc=(LD_W, 16), waits=[(VEC, None)])
                B.on("vector", lambda e: e.tensor_copy(wbf[:], w32),
                     inc=(VEC, 1), waits=[(LD_W, None), (TEN, None)])
                for g in range(4):
                    gp = g % 2
                    for j in range(4):
                        dc = g * 4 + j
                        B.on("tensor", lambda e, gp=gp, j=j, dc=dc: e.transpose(
                            PB[gp][:, j * 128:(j + 1) * 128],
                            wbf[:, dc * 128:(dc + 1) * 128], idbS[:]),
                            inc=(TEN, 1), waits=[(VEC, None)] if j == 0 else [])
                    B.on("vector", lambda e, nm=nm, g=g, gp=gp: e.tensor_copy(
                        wT[nm][:, g * 512:(g + 1) * 512], PB[gp][:]),
                        inc=(VEC, 1), waits=[(TEN, None)])
            xnb_free = [0, 0]
            for tau in range(SEQ // 512):
                p = tau % 2
                for dc in range(16):
                    B.on("sync", lambda e, p=p, dc=dc, tau=tau: e.dma_start(
                        out=xnb[p][:, dc * 512:(dc + 1) * 512],
                        in_=xnT_d[dc * 128:(dc + 1) * 128, tau * 512:(tau + 1) * 512]),
                        inc=(LD_XNB[p], 16),
                        waits=([(ST_XNT[0], None), (ST_XNT[1], None),
                                (TEN, xnb_free[p])] if dc == 0 else []))
                ld_here = LD_XNB[p].n
                for wi, nm in enumerate("qkv"):
                    bank = 3 * p + wi
                    for dc in range(16):
                        B.on("tensor", lambda e, bank=bank, nm=nm, dc=dc, p=p: e.matmul(
                            PS[bank][:], wT[nm][:, dc * 128:(dc + 1) * 128],
                            xnb[p][:, dc * 512:(dc + 1) * 512],
                            start=(dc == 0), stop=(dc == 15)),
                            inc=(TEN, 1),
                            waits=([(LD_XNB[p], ld_here), (VEC, None)] if dc == 0 else []))
                    dstT = {"q": QT, "k": KT, "v": VT}[nm]
                    B.on("vector", lambda e, dstT=dstT, tau=tau, bank=bank: e.tensor_copy(
                        dstT[:, tau * 512:(tau + 1) * 512], PS[bank][:]),
                        inc=(VEC, 1), waits=[(TEN, None)])
                xnb_free[p] = TEN.n
            for g in range(SEQ // 512):
                gp = g % 2
                for j in range(4):
                    kb = g * 4 + j
                    B.on("tensor", lambda e, gp=gp, j=j, kb=kb: e.transpose(
                        PB[gp][:, j * 128:(j + 1) * 128], VT[:, kb * 128:(kb + 1) * 128],
                        idbS[:]),
                        inc=(TEN, 1), waits=[(VEC, None)] if j == 0 else [])
                B.on("vector", lambda e, g=g, gp=gp: e.tensor_copy(
                    Vtm[:, g * 512:(g + 1) * 512], PB[gp][:]),
                    inc=(VEC, 1), waits=[(TEN, None)])
            s_free = [0, 0]
            p_free = [0, 0]
            pt_free = [0, 0]
            z_free = [0, 0]
            for qi in range(nqt):
                qp = qi % 2
                nk = qi + 1
                nb = (nk + 3) // 4
                for kb in range(nb):
                    bank = kb % 2
                    B.on("tensor", lambda e, bank=bank, qi=qi, kb=kb: e.matmul(
                        PS[bank][:], QT[:, qi * 128:(qi + 1) * 128],
                        KT[:, kb * 512:(kb + 1) * 512], start=True, stop=True),
                        inc=(TEN, 1),
                        waits=([(VEC, s_free[qp]), (SCA, None)] if kb == 0 else [(SCA, None)]))
                    B.on("scalar", lambda e, qp=qp, kb=kb, bank=bank: e.mul(
                        S_sb[qp][:, kb * 512:(kb + 1) * 512], PS[bank][:], INV_SQRT_DH),
                        inc=(SCA, 1), waits=[(TEN, None)])
                B.on("vector", lambda e, qp=qp, qi=qi: e.tensor_mul(
                    S_sb[qp][:, qi * 128:(qi + 1) * 128],
                    S_sb[qp][:, qi * 128:(qi + 1) * 128], trilS[:]),
                    inc=(VEC, 1), waits=[(SCA, None)])
                B.on("vector", lambda e, qp=qp, qi=qi: e.tensor_add(
                    S_sb[qp][:, qi * 128:(qi + 1) * 128],
                    S_sb[qp][:, qi * 128:(qi + 1) * 128], ngtS[:]),
                    inc=(VEC, 1))
                if nk % 4 != 0:
                    B.on("vector", lambda e, qp=qp, nk=nk, nb=nb: e.memset(
                        S_sb[qp][:, nk * 128:nb * 512], NEGBIG), inc=(VEC, 1))
                B.on("vector", lambda e, qp=qp, nb=nb: e.reduce_max(
                    mx[qp][:, 0:1], S_sb[qp][:, 0:nb * 512], AX, negate=True),
                    inc=(VEC, 1))
                B.on("scalar", lambda e, qp=qp, nb=nb: e.activation(
                    P_sb[qp][:, 0:nb * 512], S_sb[qp][:, 0:nb * 512], AF.Exp,
                    bias=mx[qp][:, 0:1], accum_out=sump[qp][:, 0:1]),
                    inc=(SCA, 1), waits=[(VEC, None), (TEN, p_free[qp])])
                s_free[qp] = SCA.n
                B.on("vector", lambda e, qp=qp: e.reciprocal(
                    sump[qp][:, 1:2], sump[qp][:, 0:1]),
                    inc=(VEC, 1), waits=[(SCA, None)])
                B.on("vector", lambda e, qp=qp, nb=nb: e.tensor_scalar(
                    P_sb[qp][:, 0:nb * 512], P_sb[qp][:, 0:nb * 512],
                    sump[qp][:, 1:2], None, ALU.mult),
                    inc=(VEC, 1))
                vec_pnorm = VEC.n
                ng = nb
                for g in range(ng):
                    gp2 = g % 2
                    jmax = min(4, nk - g * 4)
                    for j in range(jmax):
                        kb = g * 4 + j
                        B.on("tensor", lambda e, gp2=gp2, j=j, kb=kb, qp=qp: e.transpose(
                            PB[gp2][:, j * 128:(j + 1) * 128],
                            P_sb[qp][:, kb * 128:(kb + 1) * 128], idbS[:]),
                            inc=(TEN, 1),
                            waits=([(VEC, None)] if j == 0 else []))
                    B.on("vector", lambda e, g=g, gp2=gp2, qp=qp, jmax=jmax: e.tensor_copy(
                        PT_sb[qp][:, g * 512:g * 512 + jmax * 128],
                        PB[gp2][:, 0:jmax * 128]),
                        inc=(VEC, 1), waits=[(TEN, None)])
                p_free[qp] = TEN.n
                zbank = 2 + qp
                for kb in range(nk):
                    B.on("tensor", lambda e, zbank=zbank, kb=kb, qp=qp, nk=nk: e.matmul(
                        PS[zbank][:, 0:128], Vtm[:, kb * 128:(kb + 1) * 128],
                        PT_sb[qp][:, kb * 128:(kb + 1) * 128],
                        start=(kb == 0), stop=(kb == nk - 1)),
                        inc=(TEN, 1),
                        waits=([(VEC, None), (VEC, z_free[qp])] if kb == 0 else []))
                pt_free[qp] = TEN.n
                B.on("vector", lambda e, qp=qp, qi=qi, zbank=zbank: e.tensor_copy(
                    zT_sb[:, qi * 128:(qi + 1) * 128], PS[zbank][:, 0:128]),
                    inc=(VEC, 1), waits=[(TEN, None)])
                z_free[qp] = VEC.n
            B.on("gpsimd", lambda e, h=h: e.dma_start(
                out=zT[h * DH:(h + 1) * DH, :], in_=zT_sb[:]),
                inc=(ST_Z, 16), waits=[(VEC, None)])
            B.waitq("vector", [(ST_Z, None)])

        B.waitq("gpsimd", [(ST_Z, None), (ST_XNT[0], None), (ST_XNT[1], None)])
        B.emit()
    return nc



TOK = 2048


def build_post0():
    nc = bass.Bass("TRN2", target_bir_lowering=False)
    x_sh = nc.dram_tensor("x_sh", [TOK, D], F32, kind="ExternalInput").ap()
    zTm = nc.dram_tensor("zTm", [D, TOK], BF16, kind="ExternalInput").ap()
    xnTm = nc.dram_tensor("xnTm", [D, TOK], BF16, kind="ExternalInput").ap()
    wo = nc.dram_tensor("wo", [D, D], F32, kind="ExternalInput").ap()
    fw = nc.dram_tensor("fw", [D, D], F32, kind="ExternalInput").ap()
    fb = nc.dram_tensor("fb", [1, D], F32, kind="ExternalInput").ap()
    ones1 = nc.dram_tensor("ones1", [1, 128], F32, kind="ExternalInput").ap()
    identbf = nc.dram_tensor("identbf", [128, 128], BF16, kind="ExternalInput").ap()
    x1_sh = nc.dram_tensor("x1_sh", [TOK, D], F32, kind="ExternalOutput").ap()
    xn1T = nc.dram_tensor("xn1T", [D, TOK], BF16, kind="ExternalOutput").ap()

    with ExitStack() as ctx:
        B = Builder(nc, ctx)
        LD_C = B.sem("LD_C")
        LD_W = B.sem("LD_W")
        LD_Z = [B.sem(f"LD_Z{i}") for i in range(2)]
        LD_XN = [B.sem(f"LD_XN{i}") for i in range(2)]
        LD_X = [B.sem(f"LD_X{i}") for i in range(2)]
        ST_X1 = [B.sem(f"ST_X1{i}") for i in range(2)]
        ST_N = [B.sem(f"ST_N{i}") for i in range(2)]
        TEN = B.sem("TEN")
        VEC = B.sem("VEC")
        SCA = B.sem("SCA")
        B.set_engine_sems(vector=VEC, scalar=SCA)

        idbS = B.sb("idbS", [128, 128], BF16)
        ones1S = B.sb("ones1S", [1, 128])
        fbS = B.sb("fbS", [1, D])
        epsT = B.sb("epsT", [128, 1])
        red = B.sb("red", [128, 8])
        wbf = B.sb("wbf", [128, D], BF16)
        alph = B.sb("alph", [128, 2])
        wot = B.sb("wot", [128, 16 * D], BF16)
        bwt = B.sb("bwt", [128, 16 * D], BF16)
        biasb = B.sb("biasb", [128, D])
        zt = [B.sb(f"zt{i}", [128, D], BF16) for i in range(2)]
        xnt = [B.sb(f"xnt{i}", [128, D], BF16) for i in range(2)]
        xt = [B.sb(f"xt{i}", [128, D]) for i in range(2)]
        xacc = [B.sb(f"xacc{i}", [128, D]) for i in range(2)]
        sqs = B.sb("sqs", [128, D])
        w32 = xacc[0][:]
        sgn = sqs[:]
        xnbf = [zt[i][:] for i in range(2)]
        stg = [xnt[i][:] for i in range(2)]

        PS = [B.ps(f"PS{i}", [128, 512]) for i in range(6)]
        PB = [B.ps(f"PB{i}", [128, 512], BF16) for i in range(2)]

        B.on("vector", lambda e: e.memset(epsT[:], 1e-5), inc=(VEC, 1))
        B.on("sync", lambda e: e.dma_start(out=idbS[:], in_=identbf), inc=(LD_C, 16))
        B.on("sync", lambda e: e.dma_start(out=ones1S[:], in_=ones1), inc=(LD_C, 16))
        B.on("sync", lambda e: e.dma_start(out=fbS[:], in_=fb), inc=(LD_C, 16))
        for db in range(4):
            B.on("tensor", lambda e, db=db: e.matmul(
                PS[db][:], ones1S[:], fbS[:, db * 512:(db + 1) * 512],
                start=True, stop=True), inc=(TEN, 1), waits=[(LD_C, 48)])
            B.on("vector", lambda e, db=db: e.tensor_copy(
                biasb[:, db * 512:(db + 1) * 512], PS[db][:]),
                inc=(VEC, 1), waits=[(TEN, None)])

        def prep_w(src, dstT, binarize):
            for rc in range(16):
                B.on("sync", lambda e, src=src, rc=rc: e.dma_start(
                    out=w32, in_=src[rc * 128:(rc + 1) * 128, :]),
                    inc=(LD_W, 16), waits=[(VEC, None), (SCA, None)])
                if binarize:
                    B.on("vector", lambda e: e.reduce_sum(
                        alph[:, 0:1], w32, AX, apply_absolute_value=True),
                        inc=(VEC, 1), waits=[(LD_W, None)])
                    B.on("vector", lambda e: e.tensor_scalar(
                        alph[:, 1:2], alph[:, 0:1], 1.0 / D, None, ALU.mult),
                        inc=(VEC, 1))
                    B.on("scalar", lambda e: e.sign(sgn, w32),
                         inc=(SCA, 1), waits=[(LD_W, None)])
                    B.on("vector", lambda e: e.tensor_scalar(
                        wbf[:], sgn, alph[:, 1:2], None, ALU.mult),
                        inc=(VEC, 1), waits=[(SCA, None), (TEN, None)])
                else:
                    B.on("vector", lambda e: e.tensor_copy(wbf[:], w32),
                         inc=(VEC, 1), waits=[(LD_W, None), (TEN, None)])
                for g in range(4):
                    gp = g % 2
                    for j in range(4):
                        cc = g * 4 + j
                        B.on("tensor", lambda e, gp=gp, j=j, cc=cc: e.transpose(
                            PB[gp][:, j * 128:(j + 1) * 128],
                            wbf[:, cc * 128:(cc + 1) * 128], idbS[:]),
                            inc=(TEN, 1), waits=[(VEC, None)] if j == 0 else [])
                    B.on("vector", lambda e, g=g, gp=gp, rc=rc, dstT=dstT: e.tensor_copy(
                        dstT[:].rearrange("r (cc d) -> r cc d", d=D)
                        [:, g * 4:(g + 1) * 4, rc * 128:(rc + 1) * 128],
                        PB[gp][:].rearrange("r (cc d) -> r cc d", d=128)),
                        inc=(VEC, 1), waits=[(TEN, None)])

        prep_w(wo, wot, False)
        prep_w(fw, bwt, True)

        zt_free = [0, 0]
        xnt_free = [0, 0]
        xt_free = [0, 0]
        xn_free = [0, 0]
        stg_free = [0, 0]
        x1_free = [0, 0]
        xacc_free_s = [0, 0]
        ps_free = VEC.n
        for tt in range(TOK // 128):
            p = tt % 2
            B.on("sync", lambda e, p=p, tt=tt: e.dma_start(
                out=zt[p][:].rearrange("r (fc c) -> r fc c", c=128),
                in_=zTm[:, tt * 128:(tt + 1) * 128].rearrange("(fc r) c -> r fc c", r=128)),
                inc=(LD_Z[p], 16), waits=[(TEN, zt_free[p])])
            ldz = LD_Z[p].n
            B.on("sync", lambda e, p=p, tt=tt: e.dma_start(
                out=xnt[p][:].rearrange("r (fc c) -> r fc c", c=128),
                in_=xnTm[:, tt * 128:(tt + 1) * 128].rearrange("(fc r) c -> r fc c", r=128)),
                inc=(LD_XN[p], 16), waits=[(TEN, xnt_free[p]),
                                           (ST_N[p], stg_free[p])])
            ldxn = LD_XN[p].n
            B.on("sync", lambda e, p=p, tt=tt: e.dma_start(
                out=xt[p][:], in_=x_sh[tt * 128:(tt + 1) * 128, :]),
                inc=(LD_X[p], 16), waits=[(VEC, xt_free[p])])
            ldx = LD_X[p].n
            for fc in range(16):
                for db in range(4):
                    B.on("tensor", lambda e, p=p, fc=fc, db=db: e.matmul(
                        PS[db][:], zt[p][:, fc * 128:(fc + 1) * 128],
                        wot[:, fc * D + db * 512: fc * D + (db + 1) * 512],
                        start=(fc == 0), stop=(fc == 15)),
                        inc=(TEN, 1),
                        waits=([(LD_Z[p], ldz), (VEC, ps_free)]
                               if (fc == 0 and db == 0) else []))
            ten_attn = TEN.n
            for db in range(4):
                B.on("vector", lambda e, p=p, db=db: e.scalar_tensor_tensor(
                    xacc[p][:, db * 512:(db + 1) * 512], PS[db][:], 1.0,
                    xt[p][:, db * 512:(db + 1) * 512], ALU.mult, ALU.add),
                    inc=(VEC, 1),
                    waits=([(TEN, ten_attn), (LD_X[p], ldx),
                            (SCA, xacc_free_s[p]), (ST_X1[p], x1_free[p])]
                           if db == 0 else []))
            vec_attn = VEC.n
            for ic in range(16):
                for db in range(4):
                    B.on("tensor", lambda e, p=p, ic=ic, db=db: e.matmul(
                        PS[db][:], xnt[p][:, ic * 128:(ic + 1) * 128],
                        bwt[:, ic * D + db * 512: ic * D + (db + 1) * 512],
                        start=(ic == 0), stop=(ic == 15)),
                        inc=(TEN, 1),
                        waits=([(LD_XN[p], ldxn), (VEC, vec_attn)]
                               if (ic == 0 and db == 0) else []))
            zt_free[p] = TEN.n
            xnt_free[p] = TEN.n
            for db in range(4):
                B.on("vector", lambda e, p=p, db=db: e.scalar_tensor_tensor(
                    xacc[p][:, db * 512:(db + 1) * 512], PS[db][:], 1.0,
                    xacc[p][:, db * 512:(db + 1) * 512], ALU.mult, ALU.add),
                    inc=(VEC, 1), waits=([(TEN, None)] if db == 0 else []))
            ps_free = VEC.n
            B.on("vector", lambda e, p=p: e.tensor_add(xacc[p][:], xacc[p][:], biasb[:]),
                 inc=(VEC, 1))
            xt_free[p] = VEC.n
            B.on("gpsimd", lambda e, p=p, tt=tt: e.dma_start(
                out=x1_sh[tt * 128:(tt + 1) * 128, :], in_=xacc[p][:]),
                inc=(ST_X1[p], 16), waits=[(VEC, None)])
            x1_free[p] = ST_X1[p].n
            B.on("vector", lambda e, p=p: e.reduce_sum(red[:, 0:1], xacc[p][:], AX),
                 inc=(VEC, 1))
            B.on("scalar", lambda e, p=p: e.activation(
                sqs[:], xacc[p][:], AF.Square, accum_out=red[:, 2:3]),
                inc=(SCA, 1), waits=[(VEC, None)])
            B.on("vector", lambda e: e.tensor_scalar(
                red[:, 1:2], red[:, 0:1], 1.0 / D, None, ALU.mult), inc=(VEC, 1))
            B.on("vector", lambda e: e.tensor_mul(red[:, 3:4], red[:, 1:2], red[:, 1:2]),
                 inc=(VEC, 1))
            B.on("vector", lambda e: e.tensor_scalar(
                red[:, 4:5], red[:, 2:3], 1.0 / D, None, ALU.mult),
                inc=(VEC, 1), waits=[(SCA, None)])
            B.on("vector", lambda e: e.tensor_sub(red[:, 4:5], red[:, 4:5], red[:, 3:4]),
                 inc=(VEC, 1))
            B.on("scalar", lambda e: e.activation(red[:, 5:6], red[:, 4:5], AF.Ln,
                                                  bias=epsT[:]),
                 inc=(SCA, 1), waits=[(VEC, None)])
            B.on("scalar", lambda e: e.activation(red[:, 6:7], red[:, 5:6], AF.Exp,
                                                  scale=-0.5),
                 inc=(SCA, 1))
            B.on("vector", lambda e, p=p: e.tensor_scalar(
                xnbf[p][:], xacc[p][:], red[:, 1:2], red[:, 6:7],
                ALU.subtract, ALU.mult),
                inc=(VEC, 1), waits=[(SCA, None), (TEN, xn_free[p])])
            xacc_free_s[p] = SCA.n
            for g in range(4):
                gp = g % 2
                for j in range(4):
                    dc = g * 4 + j
                    B.on("tensor", lambda e, p=p, gp=gp, j=j, dc=dc: e.transpose(
                        PB[gp][:, j * 128:(j + 1) * 128],
                        xnbf[p][:, dc * 128:(dc + 1) * 128], idbS[:]),
                        inc=(TEN, 1), waits=[(VEC, None)] if j == 0 else [])
                B.on("vector", lambda e, g=g, gp=gp, p=p: e.tensor_copy(
                    stg[p][:, g * 512:(g + 1) * 512], PB[gp][:]),
                    inc=(VEC, 1),
                    waits=([(TEN, None), (ST_N[p], stg_free[p])] if g == 0
                           else [(TEN, None)]))
            dstn = xn1T[:, tt * 128:(tt + 1) * 128].rearrange("(dc r) c -> r dc c", r=128)
            B.on("gpsimd", lambda e, p=p, dstn=dstn: e.dma_start(
                out=dstn, in_=stg[p][:].rearrange("r (dc c) -> r dc c", c=128)),
                inc=(ST_N[p], 16), waits=[(VEC, None)])
            stg_free[p] = ST_N[p].n
            xn_free[p] = TEN.n
            zt_free[p] = TEN.n
            xnt_free[p] = TEN.n

        B.waitq("gpsimd", [(ST_X1[0], None), (ST_X1[1], None),
                           (ST_N[0], None), (ST_N[1], None)])
        B.emit()
    return nc



HL = 32
P_ = 64
N_ = 128
NCH = 32
NOUT = 4384
GBASE = 2336
DCLAMP = 60.0


def _oc_geom(oc):
    if oc < 16:
        return oc * 128, 128
    if oc == 16:
        return 2048, 128
    if oc == 17:
        return 2176, 128
    if oc == 18:
        return 2304, 32
    return GBASE + (oc - 19) * 128, 128


def build_mamba():
    nc = bass.Bass("TRN2", target_bir_lowering=False)
    xn1Tb = nc.dram_tensor("xn1Tb", [D, SEQ], BF16, kind="ExternalInput").ap()
    inw = nc.dram_tensor("inw", [D, NOUT], F32, kind="ExternalInput").ap()
    convw = nc.dram_tensor("convw", [2304, 4], F32, kind="ExternalInput").ap()
    convb = nc.dram_tensor("convb", [2304, 1], F32, kind="ExternalInput").ap()
    dtb = nc.dram_tensor("dtb", [HL, 1], F32, kind="ExternalInput").ap()
    alog = nc.dram_tensor("alog", [HL, 1], F32, kind="ExternalInput").ap()
    dpar = nc.dram_tensor("dpar", [HL, 1], F32, kind="ExternalInput").ap()
    sel = nc.dram_tensor("sel", [HL, HL * P_], F32, kind="ExternalInput").ap()
    ones1 = nc.dram_tensor("ones1", [1, 128], F32, kind="ExternalInput").ap()
    identbf = nc.dram_tensor("identbf", [128, 128], BF16, kind="ExternalInput").ap()
    hT_sh = nc.dram_tensor("hT_sh", [HL * P_, SEQ], BF16, kind="ExternalOutput").ap()
    inwbf_d = nc.dram_tensor("inwbf_d", [D, NOUT], BF16, kind="Internal").ap()
    projT_d = nc.dram_tensor("projT_d", [NOUT, SEQ], BF16, kind="Internal").ap()
    hbcs_d = nc.dram_tensor("hbcs_d", [2048, SEQ], BF16, kind="Internal").ap()
    LH0_d = nc.dram_tensor("LH0_d", [NCH, SEQ], F32, kind="Internal").ap()
    RH1_d = nc.dram_tensor("RH1_d", [NCH, SEQ], F32, kind="Internal").ap()
    ones4k_d = nc.dram_tensor("ones4k_d", [1, SEQ], F32, kind="ExternalInput").ap()

    HP = HL * P_

    with ExitStack() as ctx:
        B = Builder(nc, ctx)
        LD_C = B.sem("LD_C")
        LD_WP = B.sem("LD_WP")
        ST_WB = B.sem("ST_WB")
        LD_XIN = B.sem("LD_XIN")
        LD_WIN = [B.sem(f"LD_WIN{i}") for i in range(2)]
        ST_PJ = [B.sem(f"ST_PJ{i}") for i in range(2)]
        LD_CV = B.sem("LD_CV")
        ST_HB = B.sem("ST_HB")
        LD_DT = B.sem("LD_DT")
        ST_FL = B.sem("ST_FL")
        LD_XC = [B.sem(f"LD_XC{i}") for i in range(2)]
        LD_GT = [B.sem(f"LD_GT{i}") for i in range(2)]
        LD_LR = B.sem("LD_LR")
        ST_H = [B.sem(f"ST_H{i}") for i in range(2)]
        TEN = B.sem("TEN")
        VEC = B.sem("VEC")
        SCA = B.sem("SCA")
        GPS = B.sem("GPS")
        B.set_engine_sems(vector=VEC, scalar=SCA, gpsimd=GPS)

        idbS = B.sb("idbS", [128, 128], BF16)
        ones1S = B.sb("ones1S", [1, 128])
        ones128w = B.sb("ones128w", [128, 128])
        consts3 = B.sb("consts3", [128, 4])
        SELs = B.sb("SELs", [HL, HL * P_])
        LHt = B.sb("LHt", [2, SEQ])
        RHt = B.sb("RHt", [2, SEQ])
        DTP = B.sb("DTP", [HL, 768])
        dtc_bf = B.sb("dtc_bf", [HL, 128], BF16)
        CDR1 = B.sb("CDR1", [1, 512])
        convacc = B.sb("convacc", [128, SEQ])
        gates = B.sb("gates", [128, 2 * HL * P_], BF16)
        convin = B.sb("convin", [128, SEQ + 4], BF16)
        convwS = B.sb("convwS", [128, 4])
        convbS = B.sb("convbS", [128, 1])
        xin = B.sb("xin", [128, 16 * 512], BF16)
        convs_bf = xin[:, 0:SEQ]
        winb = [B.sb(f"winb{i}", [128, 16 * 128], BF16) for i in range(2)]
        stgp = [B.sb(f"stgp{i}", [128, 512], BF16) for i in range(2)]
        BT_sb = B.sb("BT_sb", [128, SEQ], BF16)
        CT_sb = B.sb("CT_sb", [128, SEQ], BF16)
        S32 = B.sb("S32", [128, HP])
        Sbf = B.sb("Sbf", [128, HP], BF16)
        xch = [B.sb(f"xch{i}", [128, HP], BF16) for i in range(2)]
        x_tm = [B.sb(f"x_tm{i}", [128, HP], BF16) for i in range(2)]
        B_tm = B.sb("B_tm", [128, 128], BF16)
        w_up = B.sb("w_up", [128, HP], BF16)
        E_up = B.sb("E_up", [128, HP], BF16)
        xw = B.sb("xw", [128, HP], BF16)
        Dupb = B.sb("Dupb", [128, HP], BF16)
        Dx = B.sb("Dx", [128, HP], BF16)
        ST_c = B.sb("ST_c", [128, 128], BF16)
        Dm4 = B.sb("Dm4", [128, 512])
        G4 = B.sb("G4", [128, 512], BF16)
        M_sb = B.sb("M_sb", [128, HL * 128], BF16)
        t1_sb = B.sb("t1_sb", [128, 512])
        yh_tm = B.sb("yh_tm", [128, HP], BF16)
        hstage = [B.sb(f"hstage{i}", [128, HP], BF16) for i in range(2)]

        gate_sb = gates[:, 0:HP]
        silug = gates[:, HP:2 * HP]
        HN = NOUT // 2
        wp32 = convacc[:, 0:HN]
        wpbf = gates[:, 0:HN]
        c_u = DTP[:, 0:128]
        c_r = DTP[:, 128:256]
        c_dt = DTP[:, 256:384]
        c_cum = DTP[:, 384:512]
        c_last = DTP[:, 511:512]
        c_lnd = DTP[:, 512:640]
        c_w = DTP[:, 640:768]

        PS = [B.ps(f"PS{i}", [128, 512]) for i in range(6)]
        PB = [B.ps(f"PB{i}", [128, 512], BF16) for i in range(2)]

        B.on("vector", lambda e: e.memset(ones128w[:], 1.0), inc=(VEC, 1))
        B.on("sync", lambda e: e.dma_start(out=idbS[:], in_=identbf), inc=(LD_C, 16))
        B.on("sync", lambda e: e.dma_start(out=ones1S[:], in_=ones1), inc=(LD_C, 16))
        B.on("sync", lambda e: e.dma_start(out=SELs[:], in_=sel), inc=(LD_C, 16))
        B.on("sync", lambda e: e.dma_start(out=consts3[0:HL, 0:1], in_=dtb), inc=(LD_C, 16))
        B.on("sync", lambda e: e.dma_start(out=consts3[0:HL, 1:2], in_=alog),
             inc=(LD_C, 16))
        B.on("sync", lambda e: e.dma_start(out=consts3[0:HL, 3:4], in_=dpar),
             inc=(LD_C, 16))
        B.on("sync", lambda e: e.dma_start(out=LHt[1:2, :], in_=ones4k_d), inc=(LD_C, 16))
        B.on("sync", lambda e: e.dma_start(out=RHt[0:1, :], in_=ones4k_d), inc=(LD_C, 16))
        ldc_all = LD_C.n
        B.on("scalar", lambda e: e.activation(
            consts3[0:HL, 2:3], consts3[0:HL, 1:2], AF.Exp),
            inc=(SCA, 1), waits=[(LD_C, ldc_all)])
        B.on("vector", lambda e: e.tensor_scalar(
            consts3[0:HL, 2:3], consts3[0:HL, 2:3], -1.0, None, ALU.mult),
            inc=(VEC, 1), waits=[(SCA, None)])
        for blk in range(4):
            B.on("tensor", lambda e, blk=blk: e.matmul(
                PS[0][0:1, :], consts3[0:HL, 3:4],
                SELs[:, blk * 512:(blk + 1) * 512],
                start=True, stop=True), inc=(TEN, 1),
                waits=[(LD_C, ldc_all), (VEC, None)] if blk == 0
                else [(VEC, None)])
            B.on("vector", lambda e, blk=blk: e.tensor_copy(
                t1_sb[0:1, :], PS[0][0:1, :]), inc=(VEC, 1), waits=[(TEN, None)])
            B.on("tensor", lambda e, blk=blk: e.matmul(
                PS[1][:], ones1S[:], t1_sb[0:1, :], start=True, stop=True),
                inc=(TEN, 1), waits=[(VEC, None)])
            B.on("vector", lambda e, blk=blk: e.tensor_copy(
                Dupb[:, blk * 512:(blk + 1) * 512], PS[1][:]),
                inc=(VEC, 1), waits=[(TEN, None)])

        for rc in range(16):
            for hh in range(2):
                B.on("sync", lambda e, rc=rc, hh=hh: e.dma_start(
                    out=wp32, in_=inw[rc * 128:(rc + 1) * 128,
                                      hh * HN:(hh + 1) * HN]),
                    inc=(LD_WP, 16), waits=[(VEC, None)])
                B.on("vector", lambda e: e.tensor_copy(wpbf, wp32),
                     inc=(VEC, 1), waits=[(LD_WP, None), (ST_WB, None)])
                B.on("gpsimd", lambda e, rc=rc, hh=hh: e.dma_start(
                    out=inwbf_d[rc * 128:(rc + 1) * 128, hh * HN:(hh + 1) * HN],
                    in_=wpbf),
                    inc=(ST_WB, 16), waits=[(VEC, None)])

        win_free = [0, 0]
        stgp_free = [0, 0]
        for tb in range(SEQ // 512):
            for dc in range(16):
                B.on("sync", lambda e, dc=dc, tb=tb: e.dma_start(
                    out=xin[:, dc * 512:(dc + 1) * 512],
                    in_=xn1Tb[dc * 128:(dc + 1) * 128, tb * 512:(tb + 1) * 512]),
                    inc=(LD_XIN, 16),
                    waits=([(TEN, None)] if dc == 0 else []))
            ldxin = LD_XIN.n
            for oc in range(35):
                rowbase, rows = _oc_geom(oc)
                p = oc % 2
                src = inwbf_d[:, rowbase:rowbase + rows].rearrange(
                    "(dc r) c -> r dc c", r=128)
                B.on("sync", lambda e, p=p, src=src, rows=rows: e.dma_start(
                    out=winb[p][:].rearrange("r (dc c) -> r dc c", c=128)
                    [:, :, 0:rows], in_=src),
                    inc=(LD_WIN[p], 16),
                    waits=[(TEN, win_free[p]), (ST_WB, None)])

                ldwin = LD_WIN[p].n
                for dc in range(16):
                    B.on("tensor", lambda e, p=p, dc=dc, rows=rows: e.matmul(
                        PS[p][0:rows, :], winb[p][:, dc * 128:dc * 128 + rows],
                        xin[:, dc * 512:(dc + 1) * 512],
                        start=(dc == 0), stop=(dc == 15)),
                        inc=(TEN, 1),
                        waits=([(LD_WIN[p], ldwin), (LD_XIN, ldxin), (VEC, None)]
                               if dc == 0 else []))
                win_free[p] = TEN.n
                B.on("vector", lambda e, p=p, rows=rows: e.tensor_copy(
                    stgp[p][0:rows, :], PS[p][0:rows, :]),
                    inc=(VEC, 1), waits=[(TEN, None), (ST_PJ[p], stgp_free[p])])
                B.on("gpsimd", lambda e, p=p, rows=rows, rowbase=rowbase, tb=tb:
                     e.dma_start(
                         out=projT_d[rowbase:rowbase + rows, tb * 512:(tb + 1) * 512],
                         in_=stgp[p][0:rows, :]),
                     inc=(ST_PJ[p], 16), waits=[(VEC, None)])
                stgp_free[p] = ST_PJ[p].n

        B.on("vector", lambda e: e.memset(convin[:, 0:3], 0.0), inc=(VEC, 1))
        for rc in range(18):
            B.on("sync", lambda e, rc=rc: e.dma_start(
                out=convwS[:, :], in_=convw[rc * 128:(rc + 1) * 128, :]),
                inc=(LD_CV, 16), waits=[(VEC, None), (SCA, None)])
            B.on("sync", lambda e, rc=rc: e.dma_start(
                out=convbS[:, :], in_=convb[rc * 128:(rc + 1) * 128, :]),
                inc=(LD_CV, 16))
            B.on("sync", lambda e, rc=rc: e.dma_start(
                out=convin[:, 3:SEQ + 3], in_=projT_d[rc * 128:(rc + 1) * 128, :]),
                inc=(LD_CV, 16),
                waits=([(ST_PJ[0], None), (ST_PJ[1], None)] if rc == 0 else []))
            ldcv = LD_CV.n
            B.on("vector", lambda e: e.tensor_scalar(
                convacc[:, 0:SEQ], convin[:, 0:SEQ], convwS[:, 0:1], None, ALU.mult),
                inc=(VEC, 1), waits=[(LD_CV, ldcv), (SCA, None)])
            for j in range(1, 4):
                B.on("vector", lambda e, j=j: e.scalar_tensor_tensor(
                    convacc[:, 0:SEQ], convin[:, j:SEQ + j], convwS[:, j:j + 1],
                    convacc[:, 0:SEQ], ALU.mult, ALU.add),
                    inc=(VEC, 1))
            if rc < 16:
                dst = convs_bf
            elif rc == 16:
                dst = BT_sb[:]
            else:
                dst = CT_sb[:]
            B.on("scalar", lambda e, dst=dst: e.activation(
                dst, convacc[:, 0:SEQ], AF.Silu, bias=convbS[:, 0:1]),
                inc=(SCA, 1), waits=[(VEC, None), (ST_HB, None)])
            if rc < 16:
                B.on("gpsimd", lambda e, rc=rc: e.dma_start(
                    out=hbcs_d[rc * 128:(rc + 1) * 128, :], in_=convs_bf),
                    inc=(ST_HB, 16), waits=[(SCA, None)])

        sca_conv = SCA.n
        B.on("vector", lambda e: e.memset(S32[:], 0.0), inc=(VEC, 1))

        xch_free = [0, 0]
        gt_free = [0, 0]
        hst_free = [0, 0]
        lhrh_free = 0
        dtc_free = 0
        lr_done = 0
        for c in range(NCH):
            p = c % 2
            B.on("sync", lambda e, c=c: e.dma_start(
                out=dtc_bf[:], in_=projT_d[2304:2336, c * 128:(c + 1) * 128]),
                inc=(LD_DT, 16),
                waits=[(VEC, dtc_free)] + ([(ST_PJ[0], None), (ST_PJ[1], None)]
                                           if c == 0 else []))
            lddt = LD_DT.n
            B.on("vector", lambda e: e.tensor_scalar(
                c_u, dtc_bf[:], consts3[0:HL, 0:1], None, ALU.add),
                inc=(VEC, 1), waits=[(LD_DT, lddt), (LD_C, ldc_all), (SCA, None)])
            dtc_free = VEC.n
            B.on("vector", lambda e: e.tensor_scalar(
                c_r, c_u, 0.0, None, ALU.max), inc=(VEC, 1))
            B.on("scalar", lambda e: e.activation(c_u, c_u, AF.Abs),
                 inc=(SCA, 1), waits=[(VEC, None)])
            B.on("scalar", lambda e: e.activation(c_u, c_u, AF.Exp, scale=-1.0),
                 inc=(SCA, 1))
            B.on("scalar", lambda e: e.activation(c_u, c_u, AF.Ln,
                                                  bias=ones128w[0:HL, 0:1]),
                 inc=(SCA, 1))
            B.on("vector", lambda e: e.tensor_add(c_dt, c_r, c_u),
                 inc=(VEC, 1), waits=[(SCA, None)])
            B.on("scalar", lambda e: e.activation(c_lnd, c_dt, AF.Ln),
                 inc=(SCA, 1), waits=[(VEC, None)])
            B.on("vector", lambda e: e.tensor_scalar(
                c_r, c_dt, consts3[0:HL, 2:3], None, ALU.mult),
                inc=(VEC, 1))
            B.on("vector", lambda e: e.tensor_tensor_scan(
                c_cum, ones128w[0:HL, :], c_r, 0.0, ALU.mult, ALU.add),
                inc=(VEC, 1))
            B.on("vector", lambda e: e.tensor_sub(c_lnd, c_lnd, c_cum),
                 inc=(VEC, 1), waits=[(SCA, None)])
            B.on("gpsimd", lambda e, c=c: e.dma_start(
                out=LH0_d[c:c + 1, :].rearrange("o (h l) -> (o h) l", l=128),
                in_=c_lnd),
                inc=(ST_FL, 16), waits=[(VEC, None), (LD_LR, lr_done)])
            B.on("gpsimd", lambda e, c=c: e.dma_start(
                out=RH1_d[c:c + 1, :].rearrange("o (h l) -> (o h) l", l=128),
                in_=c_cum),
                inc=(ST_FL, 16))
            B.on("sync", lambda e, c=c: e.dma_start(
                out=LHt[0:1, :], in_=LH0_d[c:c + 1, :]),
                inc=(LD_LR, 16), waits=[(TEN, lhrh_free), (ST_FL, None),
                                        (LD_C, ldc_all)])
            B.on("sync", lambda e, c=c: e.dma_start(
                out=RHt[1:2, :], in_=RH1_d[c:c + 1, :]),
                inc=(LD_LR, 16))
            ldlr = LD_LR.n
            lr_done = LD_LR.n
            B.on("sync", lambda e, p=p, c=c: e.dma_start(
                out=xch[p][:].rearrange("r (rc l) -> r rc l", l=128),
                in_=hbcs_d[:, c * 128:(c + 1) * 128].rearrange(
                    "(rc r) l -> r rc l", r=128)),
                inc=(LD_XC[p], 16), waits=[(TEN, xch_free[p]), (ST_HB, None)])
            ldxc = LD_XC[p].n
            B.on("sync", lambda e, p=p, c=c: e.dma_start(
                out=gate_sb.rearrange("r (rc l) -> r rc l", l=128),
                in_=projT_d[GBASE:GBASE + HP, c * 128:(c + 1) * 128].rearrange(
                    "(rc r) l -> r rc l", r=128)),
                inc=(LD_GT[p], 16), waits=[(SCA, gt_free[p])])
            ldgt = LD_GT[p].n
            for g in range(4):
                gp = g % 2
                for j in range(4):
                    rc = g * 4 + j
                    B.on("tensor", lambda e, p=p, gp=gp, j=j, rc=rc: e.transpose(
                        PB[gp][:, j * 128:(j + 1) * 128],
                        xch[p][:, rc * 128:(rc + 1) * 128], idbS[:]),
                        inc=(TEN, 1),
                        waits=([(LD_XC[p], ldxc), (VEC, None)] if j == 0
                               else []))
                B.on("vector", lambda e, g=g, gp=gp, p=p: e.tensor_copy(
                    x_tm[p][:, g * 512:(g + 1) * 512], PB[gp][:]),
                    inc=(VEC, 1), waits=[(TEN, None)])
            B.on("tensor", lambda e, c=c: e.transpose(
                PB[0][:, 0:128], BT_sb[:, c * 128:(c + 1) * 128], idbS[:]),
                inc=(TEN, 1), waits=[(VEC, None), (SCA, sca_conv)])
            B.on("vector", lambda e: e.tensor_copy(B_tm[:], PB[0][:, 0:128]),
                 inc=(VEC, 1), waits=[(TEN, None)])
            B.on("vector", lambda e: e.tensor_scalar(
                c_w, c_cum, c_last, None, ALU.subtract), inc=(VEC, 1))
            B.on("scalar", lambda e: e.activation(c_w, c_w, AF.Exp, scale=-1.0),
                 inc=(SCA, 1), waits=[(VEC, None)])
            B.on("vector", lambda e: e.tensor_mul(c_w, c_w, c_dt),
                 inc=(VEC, 1), waits=[(SCA, None)])
            for blk in range(4):
                B.on("tensor", lambda e, blk=blk: e.matmul(
                    PS[2][:], c_w, SELs[:, blk * 512:(blk + 1) * 512],
                    start=True, stop=True),
                    inc=(TEN, 1), waits=[(VEC, None), (SCA, None)])
                B.on("vector", lambda e, blk=blk: e.tensor_copy(
                    w_up[:, blk * 512:(blk + 1) * 512], PS[2][:]),
                    inc=(VEC, 1), waits=[(TEN, None)])
            for blk in range(4):
                B.on("tensor", lambda e, blk=blk: e.matmul(
                    PS[2][:], c_cum, SELs[:, blk * 512:(blk + 1) * 512],
                    start=True, stop=True),
                    inc=(TEN, 1), waits=[(VEC, None), (SCA, None)])
                B.on("scalar", lambda e, blk=blk: e.activation(
                    E_up[:, blk * 512:(blk + 1) * 512], PS[2][:], AF.Exp),
                    inc=(SCA, 1), waits=[(TEN, None), (VEC, None)])
            B.on("vector", lambda e, p=p: e.tensor_mul(xw[:], x_tm[p][:], w_up[:]),
                 inc=(VEC, 1), waits=[(SCA, None), (TEN, None)])
            B.on("vector", lambda e: e.tensor_copy(Sbf[:], S32[:]),
                 inc=(VEC, 1), waits=[(TEN, None)])
            for blk in range(4):
                B.on("tensor", lambda e, blk=blk: e.matmul(
                    PS[2][0:1, :], c_last, SELs[:, blk * 512:(blk + 1) * 512],
                    start=True, stop=True),
                    inc=(TEN, 1), waits=[(VEC, None), (SCA, None)])
                B.on("scalar", lambda e, blk=blk: e.activation(
                    CDR1[:], PS[2][0:1, :], AF.Exp),
                    inc=(SCA, 1), waits=[(TEN, None)])
                B.on("tensor", lambda e, blk=blk: e.matmul(
                    PS[3][:], ones1S[:], CDR1[:], start=True, stop=True),
                    inc=(TEN, 1), waits=[(SCA, None)])
                B.on("tensor", lambda e, blk=blk: e.matmul(
                    PS[4][:], B_tm[:], xw[:, blk * 512:(blk + 1) * 512],
                    start=True, stop=True),
                    inc=(TEN, 1))
                B.on("vector", lambda e, blk=blk: e.tensor_mul(
                    S32[:, blk * 512:(blk + 1) * 512],
                    S32[:, blk * 512:(blk + 1) * 512], PS[3][:]),
                    inc=(VEC, 1), waits=[(TEN, None)])
                B.on("vector", lambda e, blk=blk: e.tensor_add(
                    S32[:, blk * 512:(blk + 1) * 512],
                    S32[:, blk * 512:(blk + 1) * 512], PS[4][:]),
                    inc=(VEC, 1))
            B.on("tensor", lambda e, c=c: e.matmul(
                PS[5][0:128, 0:128], BT_sb[:, c * 128:(c + 1) * 128],
                CT_sb[:, c * 128:(c + 1) * 128], start=True, stop=True),
                inc=(TEN, 1), waits=[(VEC, None), (SCA, sca_conv)])
            B.on("vector", lambda e: e.tensor_copy(ST_c[:], PS[5][:, 0:128]),
                 inc=(VEC, 1), waits=[(TEN, None)])
            strep = ST_c[:].unsqueeze(1).broadcast_to([128, 4, 128])
            for hg in range(8):
                pd = hg % 2
                for j in range(4):
                    h = hg * 4 + j
                    B.on("tensor", lambda e, pd=pd, j=j, h=h: e.matmul(
                        PS[pd][:, j * 128:(j + 1) * 128],
                        LHt[:, h * 128:(h + 1) * 128], RHt[:, h * 128:(h + 1) * 128],
                        start=True, stop=True),
                        inc=(TEN, 1),
                        waits=([(LD_LR, ldlr), (VEC, None)] if (hg == 0 and j == 0)
                               else ([(VEC, None)] if j == 0 else [])))
                B.on("vector", lambda e, pd=pd: e.tensor_scalar(
                    Dm4[:], PS[pd][:], DCLAMP, None, ALU.min),
                    inc=(VEC, 1), waits=[(TEN, None), (SCA, None)])
                B.on("scalar", lambda e: e.activation(G4[:], Dm4[:], AF.Exp),
                     inc=(SCA, 1), waits=[(VEC, None)])
                B.on("vector", lambda e, hg=hg: e.tensor_tensor(
                    M_sb[:, hg * 512:(hg + 1) * 512].rearrange(
                        "r (j i) -> r j i", i=128),
                    G4[:].rearrange("r (j i) -> r j i", i=128),
                    strep, ALU.mult),
                    inc=(VEC, 1), waits=[(SCA, None), (TEN, None)])
                B.on("gpsimd", lambda e, hg=hg: e.affine_select(
                    M_sb[:, hg * 512:(hg + 1) * 512].rearrange(
                        "r (j i) -> r j i", i=128),
                    M_sb[:, hg * 512:(hg + 1) * 512].rearrange(
                        "r (j i) -> r j i", i=128),
                    [[0, 4], [1, 128]], ALU.is_ge, 0.0,
                    base=0, channel_multiplier=-1),
                    inc=(GPS, 1), waits=[(VEC, None)])
            gps_m = GPS.n
            lhrh_free = TEN.n
            B.on("vector", lambda e, p=p: e.tensor_mul(Dx[:], x_tm[p][:], Dupb[:]),
                 inc=(VEC, 1))
            for yg in range(4):
                for j in range(8):
                    h = yg * 8 + j
                    B.on("tensor", lambda e, p=p, j=j, h=h: e.matmul(
                        PS[3][:, j * 64:(j + 1) * 64],
                        M_sb[:, h * 128:(h + 1) * 128],
                        x_tm[p][:, h * 64:(h + 1) * 64],
                        start=True, stop=True),
                        inc=(TEN, 1),
                        waits=([(GPS, gps_m), (VEC, None)] if j == 0 else []))
                B.on("tensor", lambda e, c=c, yg=yg: e.matmul(
                    PS[5][:], CT_sb[:, c * 128:(c + 1) * 128],
                    Sbf[:, yg * 512:(yg + 1) * 512], start=True, stop=True),
                    inc=(TEN, 1))
                B.on("vector", lambda e, yg=yg: e.tensor_mul(
                    t1_sb[:], PS[5][:], E_up[:, yg * 512:(yg + 1) * 512]),
                    inc=(VEC, 1), waits=[(TEN, None)])
                B.on("vector", lambda e, yg=yg: e.scalar_tensor_tensor(
                    yh_tm[:, yg * 512:(yg + 1) * 512], PS[3][:, :], 1.0,
                    t1_sb[:], ALU.mult, ALU.add),
                    inc=(VEC, 1))
                B.on("vector", lambda e, yg=yg: e.tensor_add(
                    yh_tm[:, yg * 512:(yg + 1) * 512],
                    yh_tm[:, yg * 512:(yg + 1) * 512],
                    Dx[:, yg * 512:(yg + 1) * 512]),
                    inc=(VEC, 1))
            B.on("scalar", lambda e: e.activation(silug, gate_sb, AF.Silu),
                 inc=(SCA, 1), waits=[(LD_GT[p], ldgt), (VEC, None)])
            gt_free[p] = SCA.n
            for g in range(4):
                gp = g % 2
                for j in range(4):
                    rc = g * 4 + j
                    B.on("tensor", lambda e, gp=gp, j=j, rc=rc: e.transpose(
                        PB[gp][:, j * 128:(j + 1) * 128],
                        yh_tm[:, rc * 128:(rc + 1) * 128], idbS[:]),
                        inc=(TEN, 1), waits=([(VEC, None), (SCA, None)]
                                             if j == 0 else []))
                B.on("vector", lambda e, g=g, gp=gp, p=p: e.tensor_mul(
                    hstage[p][:, g * 512:(g + 1) * 512], PB[gp][:],
                    silug[:, g * 512:(g + 1) * 512]),
                    inc=(VEC, 1), waits=([(TEN, None), (ST_H[p], hst_free[p])]
                                         if g == 0 else [(TEN, None)]))
            B.on("gpsimd", lambda e, p=p, c=c: e.dma_start(
                out=hT_sh[:, c * 128:(c + 1) * 128].rearrange(
                    "(rc r) l -> r rc l", r=128),
                in_=hstage[p][:].rearrange("r (rc l) -> r rc l", l=128)),
                inc=(ST_H[p], 16), waits=[(VEC, None)])
            hst_free[p] = ST_H[p].n
            xch_free[p] = TEN.n

        B.waitq("gpsimd", [(ST_H[0], None), (ST_H[1], None)])
        B.emit()
    return nc



INTER = 5632
NIC = INTER // 128
POST1_PHASES = 4


def build_post1():
    nc = bass.Bass("TRN2", target_bir_lowering=False)
    x1_sh = nc.dram_tensor("x1_sh", [TOK, D], F32, kind="ExternalInput").ap()
    hTm = nc.dram_tensor("hTm", [2 * D, TOK], BF16, kind="ExternalInput").ap()
    opw = nc.dram_tensor("opw", [2 * D, D], F32, kind="ExternalInput").ap()
    nw = nc.dram_tensor("nw", [128, 32], F32, kind="ExternalInput").ap()
    gwT = nc.dram_tensor("gwT", [D, INTER], F32, kind="ExternalInput").ap()
    uwT = nc.dram_tensor("uwT", [D, INTER], F32, kind="ExternalInput").ap()
    dwT = nc.dram_tensor("dwT", [INTER, D], F32, kind="ExternalInput").ap()
    gs = nc.dram_tensor("gs", [128, NIC], F32, kind="ExternalInput").ap()
    us = nc.dram_tensor("us", [128, NIC], F32, kind="ExternalInput").ap()
    dsr = nc.dram_tensor("dsr", [1, D], F32, kind="ExternalInput").ap()
    xn1Tm = nc.dram_tensor("xn1Tm", [D, TOK], BF16, kind="ExternalInput").ap()
    ones1 = nc.dram_tensor("ones1", [1, 128], F32, kind="ExternalInput").ap()
    out_sh = nc.dram_tensor("out_sh", [TOK, D], F32, kind="ExternalOutput").ap()
    gwT_d = nc.dram_tensor("gwT_d", [D, INTER], BF16, kind="Internal").ap()
    uwT_d = nc.dram_tensor("uwT_d", [D, INTER], BF16, kind="Internal").ap()
    dwT_d = nc.dram_tensor("dwT_d", [INTER, D], BF16, kind="Internal").ap()
    opn_d = nc.dram_tensor("opn_d", [2 * D, D], BF16, kind="Internal").ap()
    xacc_d = nc.dram_tensor("xacc_d", [TOK, D], F32, kind="Internal").ap()

    with ExitStack() as ctx:
        B = Builder(nc, ctx)
        LD_C = B.sem("LD_C")
        LD_PW = B.sem("LD_PW")
        ST_PW = B.sem("ST_PW")
        LD_HT = B.sem("LD_HT")
        LD_HN = [B.sem(f"LD_HN{i}") for i in range(2)]
        LD_OP = [B.sem(f"LD_OP{i}") for i in range(2)]
        LD_X1 = [B.sem(f"LD_X1{i}") for i in range(2)]
        ST_XA = [B.sem(f"ST_XA{i}") for i in range(4)]
        LD_XI = B.sem("LD_XI")
        LD_GW = [B.sem(f"LD_GW{i}") for i in range(2)]
        LD_DW = [B.sem(f"LD_DW{i}") for i in range(2)]
        LD_XA = [B.sem(f"LD_XA{i}") for i in range(2)]
        ST_O = [B.sem(f"ST_O{i}") for i in range(2)]
        TEN = B.sem("TEN")
        VEC = B.sem("VEC")
        SCA = B.sem("SCA")
        B.set_engine_sems(vector=VEC, scalar=SCA)

        ones1S = B.sb("ones1S", [1, 128])
        onescol = B.sb("onescol", [128, 1])
        epsT = B.sb("epsT", [128, 1])
        nwS = B.sb("nwS", [128, 32])
        gsS = B.sb("gsS", [128, NIC])
        usS = B.sb("usS", [128, NIC])
        dsS = B.sb("dsS", [1, D])
        DS = B.sb("DS", [128, D])
        rstdr = B.sb("rstdr", [1, D])
        RST = B.sb("RST", [128, D], BF16)
        outacc = B.sb("outacc", [128, 4 * D])
        hdng = B.sb("hdng", [128, 11 * 512], BF16)
        pw32 = outacc[:, 0:INTER]
        pwbf = hdng[:, 0:INTER // 2]
        stage = [B.sb(f"stage{i}", [128, D]) for i in range(4)]
        sq = stage[0][:]
        hts = [B.sb(f"hts{i}", [128, D], BF16) for i in range(2)]
        hn = B.sb("hn", [128, 32 * 512], BF16)
        opt = [B.sb(f"opt{i}", [128, 1024], BF16) for i in range(2)]
        x1t = [B.sb(f"x1t{i}", [128, D]) for i in range(2)]
        xin = B.sb("xin", [128, 16 * 512], BF16)
        gwt = [B.sb(f"gwt{i}", [128, 16 * 128], BF16) for i in range(2)]
        dwt = [B.sb(f"dwt{i}", [128, 512], BF16) for i in range(2)]
        sg = B.sb("sg", [128, 512])
        t2 = B.sb("t2", [128, 512])
        xat = [x1t[0][:], x1t[1][:]]
        ot = [stage[0][:], stage[1][:]]

        PS = [B.ps(f"PS{i}", [128, 512]) for i in range(8)]
        pr_bank = lambda tt, db: tt * 2 + db

        B.on("vector", lambda e: e.memset(onescol[:], 1.0), inc=(VEC, 1))
        B.on("vector", lambda e: e.memset(epsT[:], 1e-5), inc=(VEC, 1))
        B.on("sync", lambda e: e.dma_start(out=ones1S[:], in_=ones1), inc=(LD_C, 16))
        B.on("sync", lambda e: e.dma_start(out=nwS[:], in_=nw), inc=(LD_C, 16))
        B.on("sync", lambda e: e.dma_start(out=gsS[:], in_=gs), inc=(LD_C, 16))
        B.on("sync", lambda e: e.dma_start(out=usS[:], in_=us), inc=(LD_C, 16))
        B.on("sync", lambda e: e.dma_start(out=dsS[:], in_=dsr), inc=(LD_C, 16))
        ldc = LD_C.n
        for db in range(4):
            B.on("tensor", lambda e, db=db: e.matmul(
                PS[0][:], ones1S[:], dsS[:, db * 512:(db + 1) * 512],
                start=True, stop=True), inc=(TEN, 1),
                waits=[(LD_C, ldc), (VEC, None)] if db == 0 else [(VEC, None)])
            B.on("vector", lambda e, db=db: e.tensor_copy(
                DS[:, db * 512:(db + 1) * 512], PS[0][:]),
                inc=(VEC, 1), waits=[(TEN, None)])

        def cast_w(src, dst, nrows, width, scale_nw=False):
            nh = (width + INTER // 2 - 1) // (INTER // 2)
            hw = width // nh if nh > 1 else width
            for rc in range(nrows // 128):
                for hh in range(nh):
                    B.on("sync", lambda e, src=src, rc=rc, hh=hh, hw=hw: e.dma_start(
                        out=pw32[:, 0:hw],
                        in_=src[rc * 128:(rc + 1) * 128, hh * hw:(hh + 1) * hw]),
                        inc=(LD_PW, 16), waits=[(VEC, None)])
                    if scale_nw:
                        B.on("vector", lambda e, rc=rc, hw=hw: e.tensor_scalar(
                            pwbf[:, 0:hw], pw32[:, 0:hw], nwS[:, rc:rc + 1],
                            None, ALU.mult),
                            inc=(VEC, 1), waits=[(LD_PW, None), (ST_PW, None),
                                                 (LD_C, ldc)])
                    else:
                        B.on("vector", lambda e, hw=hw: e.tensor_copy(
                            pwbf[:, 0:hw], pw32[:, 0:hw]),
                            inc=(VEC, 1), waits=[(LD_PW, None), (ST_PW, None)])
                    B.on("gpsimd", lambda e, dst=dst, rc=rc, hh=hh, hw=hw: e.dma_start(
                        out=dst[rc * 128:(rc + 1) * 128, hh * hw:(hh + 1) * hw],
                        in_=pwbf[:, 0:hw]),
                        inc=(ST_PW, 16), waits=[(VEC, None)])

        cast_w(gwT, gwT_d, D, INTER)
        cast_w(uwT, uwT_d, D, INTER)
        cast_w(dwT, dwT_d, INTER, D)
        cast_w(opw, opn_d, 2 * D, D, scale_nw=True)

        if POST1_PHASES < 2:
            zt_ = B.sb("dummyz", [128, D])
            B.on("vector", lambda e: e.memset(zt_[:], 0.0), inc=(VEC, 1))
            DST = B.sem("DST")
            for tt in range(TOK // 128):
                B.on("gpsimd", lambda e, tt=tt: e.dma_start(
                    out=out_sh[tt * 128:(tt + 1) * 128, :], in_=zt_[:]),
                    inc=(DST, 16), waits=[(VEC, None)])
            B.waitq("gpsimd", [(DST, None)])
            B.waitq("gpsimd", [(ST_PW, None)])
            B.emit()
            return nc
        ht_free = 0
        for rc in range(32):
            B.on("sync", lambda e, rc=rc: e.dma_start(
                out=hts[rc % 2][:], in_=hTm[rc * 128:(rc + 1) * 128, :]),
                inc=(LD_HT, 16), waits=[(SCA, ht_free)])
            ldht = LD_HT.n
            B.on("scalar", lambda e, rc=rc: e.activation(
                sq, hts[rc % 2][:], AF.Square),
                inc=(SCA, 1), waits=[(LD_HT, ldht), (VEC, None) if rc == 0 else
                                     (VEC, 0), (TEN, None) if rc == 0 else (TEN, 0)])
            ht_free = SCA.n
            for sb_ in range(4):
                B.on("tensor", lambda e, rc=rc, sb_=sb_: e.matmul(
                    PS[2 + sb_][0:1, :], onescol[:],
                    sq[:, sb_ * 512:(sb_ + 1) * 512],
                    start=(rc == 0), stop=(rc == 31)),
                    inc=(TEN, 1), waits=[(SCA, None)] if sb_ == 0 else [])
        for sb_ in range(4):
            B.on("vector", lambda e, sb_=sb_: e.tensor_scalar(
                rstdr[:, sb_ * 512:(sb_ + 1) * 512], PS[2 + sb_][0:1, :],
                1.0 / (2 * D), None, ALU.mult),
                inc=(VEC, 1), waits=[(TEN, None)])
        B.on("scalar", lambda e: e.activation(rstdr[:], rstdr[:], AF.Ln,
                                              bias=epsT[0:1, 0:1]),
             inc=(SCA, 1), waits=[(VEC, None)])
        B.on("scalar", lambda e: e.activation(rstdr[:], rstdr[:], AF.Exp, scale=-0.5),
             inc=(SCA, 1))
        for db in range(4):
            B.on("tensor", lambda e, db=db: e.matmul(
                PS[0][:], ones1S[:], rstdr[:, db * 512:(db + 1) * 512],
                start=True, stop=True), inc=(TEN, 1),
                waits=[(SCA, None), (VEC, None)])
            B.on("vector", lambda e, db=db: e.tensor_copy(
                RST[:, db * 512:(db + 1) * 512], PS[0][:]),
                inc=(VEC, 1), waits=[(TEN, None)])

        if POST1_PHASES < 3:
            zt_ = B.sb("dummyz", [128, D])
            B.on("vector", lambda e: e.memset(zt_[:], 0.0), inc=(VEC, 1))
            DST = B.sem("DST")
            for tt in range(TOK // 128):
                B.on("gpsimd", lambda e, tt=tt: e.dma_start(
                    out=out_sh[tt * 128:(tt + 1) * 128, :], in_=zt_[:]),
                    inc=(DST, 16), waits=[(VEC, None)])
            B.waitq("gpsimd", [(DST, None)])
            B.waitq("gpsimd", [(ST_PW, None)])
            B.waitq("vector", [(TEN, None)])
            B.emit()
            return nc
        hn_free = 0
        opt_free = [0, 0]
        x1_free = [0, 0]
        st_free = [0, 0, 0, 0]
        for tb in range(4):
            for rc in range(32):
                B.on("sync", lambda e, rc=rc, tb=tb: e.dma_start(
                    out=hts[rc % 2][:, 0:512],
                    in_=hTm[rc * 128:(rc + 1) * 128, tb * 512:(tb + 1) * 512]),
                    inc=(LD_HT, 16), waits=[(VEC, None)])
                ldht = LD_HT.n
                B.on("vector", lambda e, rc=rc, tb=tb: e.tensor_mul(
                    hn[:, rc * 512:(rc + 1) * 512], hts[rc % 2][:, 0:512],
                    RST[:, tb * 512:(tb + 1) * 512]),
                    inc=(VEC, 1),
                    waits=([(LD_HT, ldht), (TEN, hn_free)] if rc == 0
                           else [(LD_HT, ldht)]))
            for dbh in range(2):
                for rc in range(32):
                    B.on("sync", lambda e, rc=rc, dbh=dbh: e.dma_start(
                        out=opt[rc % 2][:],
                        in_=opn_d[rc * 128:(rc + 1) * 128,
                                  dbh * 1024:(dbh + 1) * 1024]),
                        inc=(LD_OP[rc % 2], 16),
                        waits=[(TEN, opt_free[rc % 2]), (ST_PW, None)])
                    ldop = LD_OP[rc % 2].n
                    for pr in range(8):
                        tt, db = pr // 2, pr % 2
                        B.on("tensor", lambda e, rc=rc, tt=tt, db=db, tb=tb:
                             e.matmul(
                                 PS[pr_bank(tt, db)][:],
                                 hn[:, rc * 512 + tt * 128: rc * 512 + (tt + 1) * 128],
                                 opt[rc % 2][:, db * 512:(db + 1) * 512],
                                 start=(rc == 0), stop=(rc == 31)),
                             inc=(TEN, 1),
                             waits=([(LD_OP[rc % 2], ldop), (VEC, None)]
                                    if pr == 0 else []))
                    opt_free[rc % 2] = TEN.n
                for tt in range(4):
                    tp = tt % 2
                    B.on("sync", lambda e, tp=tp, tb=tb, tt=tt: e.dma_start(
                        out=x1t[tp][:], in_=x1_sh[tb * 512 + tt * 128:
                                                  tb * 512 + (tt + 1) * 128, :]),
                        inc=(LD_X1[tp], 16), waits=[(VEC, x1_free[tp])])
                    ldx1 = LD_X1[tp].n
                    for db in range(2):
                        B.on("vector", lambda e, tp=tp, tt=tt, db=db, dbh=dbh:
                             e.scalar_tensor_tensor(
                                 stage[tt][:, dbh * 1024 + db * 512:
                                            dbh * 1024 + (db + 1) * 512],
                                 PS[pr_bank(tt, db)][:], 1.0,
                                 x1t[tp][:, dbh * 1024 + db * 512:
                                         dbh * 1024 + (db + 1) * 512],
                                 ALU.mult, ALU.add),
                             inc=(VEC, 1),
                             waits=([(TEN, None), (LD_X1[tp], ldx1),
                                     (ST_XA[tt], st_free[tt])]
                                    if db == 0 else []))
                    x1_free[tp] = VEC.n
                    if dbh == 1:
                        B.on("gpsimd", lambda e, tt=tt, tb=tb: e.dma_start(
                            out=xacc_d[tb * 512 + tt * 128:
                                       tb * 512 + (tt + 1) * 128, :],
                            in_=stage[tt][:]),
                            inc=(ST_XA[tt], 16), waits=[(VEC, None)])
                        st_free[tt] = ST_XA[tt].n
            hn_free = TEN.n

        if POST1_PHASES < 4:
            zt_ = B.sb("dummyz", [128, D])
            B.on("vector", lambda e: e.memset(zt_[:], 0.0), inc=(VEC, 1))
            DST = B.sem("DST")
            for tt in range(TOK // 128):
                B.on("gpsimd", lambda e, tt=tt: e.dma_start(
                    out=out_sh[tt * 128:(tt + 1) * 128, :], in_=zt_[:]),
                    inc=(DST, 16), waits=[(VEC, None)])
            B.waitq("gpsimd", [(DST, None)])
            B.waitq("gpsimd", [(ST_XA[0], None), (ST_XA[1], None),
                               (ST_XA[2], None), (ST_XA[3], None)])
            B.emit()
            return nc
        gw_free = [0, 0]
        dw_free = [0, 0]
        xa_free = [0, 0]
        o_free = [0, 0]
        for tb in range(4):
            for dc in range(16):
                B.on("sync", lambda e, dc=dc, tb=tb: e.dma_start(
                    out=xin[:, dc * 512:(dc + 1) * 512],
                    in_=xn1Tm[dc * 128:(dc + 1) * 128, tb * 512:(tb + 1) * 512]),
                    inc=(LD_XI, 16), waits=([(TEN, None)] if dc == 0 else []))
            ldxi = LD_XI.n
            B.on("vector", lambda e: e.memset(outacc[:], 0.0),
                 inc=(VEC, 1), waits=[(TEN, None)])
            for icg in range(4):
                for ici in range(11):
                    ic = icg * 11 + ici
                    pg = ici % 2
                    B.on("sync", lambda e, pg=pg, ic=ic: e.dma_start(
                        out=gwt[pg][:].rearrange("r (dc c) -> r dc c", c=128),
                        in_=gwT_d[:, ic * 128:(ic + 1) * 128].rearrange(
                            "(dc r) c -> r dc c", r=128)),
                        inc=(LD_GW[pg], 16),
                        waits=[(TEN, gw_free[pg]), (ST_PW, None)])
                    ldgw = LD_GW[pg].n
                    for dc in range(16):
                        B.on("tensor", lambda e, pg=pg, dc=dc: e.matmul(
                            PS[4][:], gwt[pg][:, dc * 128:(dc + 1) * 128],
                            xin[:, dc * 512:(dc + 1) * 512],
                            start=(dc == 0), stop=(dc == 15)),
                            inc=(TEN, 1),
                            waits=([(LD_GW[pg], ldgw), (LD_XI, ldxi), (VEC, None)]
                                   if dc == 0 else []))
                    B.on("sync", lambda e, pg=pg, ic=ic: e.dma_start(
                        out=gwt[pg][:].rearrange("r (dc c) -> r dc c", c=128),
                        in_=uwT_d[:, ic * 128:(ic + 1) * 128].rearrange(
                            "(dc r) c -> r dc c", r=128)),
                        inc=(LD_GW[pg], 16), waits=[(TEN, None)])
                    ldgw = LD_GW[pg].n
                    for dc in range(16):
                        B.on("tensor", lambda e, pg=pg, dc=dc: e.matmul(
                            PS[5][:], gwt[pg][:, dc * 128:(dc + 1) * 128],
                            xin[:, dc * 512:(dc + 1) * 512],
                            start=(dc == 0), stop=(dc == 15)),
                            inc=(TEN, 1),
                            waits=([(LD_GW[pg], ldgw), (VEC, None)]
                                   if dc == 0 else []))
                    gw_free[pg] = TEN.n
                    B.on("vector", lambda e, ic=ic: e.tensor_scalar(
                        sg[:], PS[4][:], gsS[:, ic:ic + 1], None, ALU.mult),
                        inc=(VEC, 1), waits=[(TEN, None)])
                    B.on("scalar", lambda e: e.activation(sg[:], sg[:], AF.Silu),
                         inc=(SCA, 1), waits=[(VEC, None)])
                    B.on("vector", lambda e, ic=ic: e.tensor_scalar(
                        t2[:], PS[5][:], usS[:, ic:ic + 1], None, ALU.mult),
                        inc=(VEC, 1))
                    B.on("vector", lambda e, ici=ici: e.tensor_mul(
                        hdng[:, ici * 512:(ici + 1) * 512], sg[:], t2[:]),
                        inc=(VEC, 1), waits=[(SCA, None), (TEN, None)])
                hdn_done = VEC.n
                for hb in range(4):
                    for ici in range(11):
                        ic = icg * 11 + ici
                        pd = ici % 2
                        B.on("sync", lambda e, pd=pd, ic=ic, hb=hb: e.dma_start(
                            out=dwt[pd][:],
                            in_=dwT_d[ic * 128:(ic + 1) * 128,
                                      hb * 512:(hb + 1) * 512]),
                            inc=(LD_DW[pd], 16),
                            waits=[(TEN, dw_free[pd]), (ST_PW, None)])
                        lddw = LD_DW[pd].n
                        for tt in range(4):
                            B.on("tensor", lambda e, pd=pd, ici=ici, tt=tt: e.matmul(
                                PS[tt][:], hdng[:, ici * 512 + tt * 128:
                                                ici * 512 + (tt + 1) * 128],
                                dwt[pd][:],
                                start=(ici == 0), stop=(ici == 10)),
                                inc=(TEN, 1),
                                waits=([(LD_DW[pd], lddw), (VEC, None)]
                                       if tt == 0 else []))
                        dw_free[pd] = TEN.n
                    for tt in range(4):
                        B.on("vector", lambda e, tt=tt, hb=hb: e.scalar_tensor_tensor(
                            outacc[:, tt * D + hb * 512: tt * D + (hb + 1) * 512],
                            PS[tt][:], 1.0,
                            outacc[:, tt * D + hb * 512: tt * D + (hb + 1) * 512],
                            ALU.mult, ALU.add),
                            inc=(VEC, 1), waits=([(TEN, None)] if tt == 0 else []))
            for tt in range(4):
                tp = tt % 2
                B.on("sync", lambda e, tp=tp, tb=tb, tt=tt: e.dma_start(
                    out=xat[tp][:], in_=xacc_d[tb * 512 + tt * 128:
                                               tb * 512 + (tt + 1) * 128, :]),
                    inc=(LD_XA[tp], 16),
                    waits=[(VEC, xa_free[tp]), (ST_XA[0], None), (ST_XA[1], None)])
                ldxa = LD_XA[tp].n
                for db in range(4):
                    B.on("vector", lambda e, tp=tp, tt=tt, db=db: e.tensor_mul(
                        ot[tp][:, db * 512:(db + 1) * 512],
                        outacc[:, tt * D + db * 512: tt * D + (db + 1) * 512],
                        DS[:, db * 512:(db + 1) * 512]),
                        inc=(VEC, 1),
                        waits=([(LD_XA[tp], ldxa), (ST_O[tp], o_free[tp])]
                               if db == 0 else []))
                B.on("vector", lambda e, tp=tp: e.tensor_add(
                    ot[tp][:], ot[tp][:], xat[tp][:]), inc=(VEC, 1))
                xa_free[tp] = VEC.n
                B.on("gpsimd", lambda e, tp=tp, tb=tb, tt=tt: e.dma_start(
                    out=out_sh[tb * 512 + tt * 128: tb * 512 + (tt + 1) * 128, :],
                    in_=ot[tp][:]),
                    inc=(ST_O[tp], 16), waits=[(VEC, None)])
                o_free[tp] = ST_O[tp].n

        B.waitq("gpsimd", [(ST_O[0], None), (ST_O[1], None)])
        B.emit()
    return nc



FUSED_PHASES = 4
FUSED_P1SUB = 4
NHF = 16
HLF = 64
HPF = HLF * P_
NOUTF = 8512
GBASEF = 4416


def _oc_geom_f(oc):
    if oc < 32:
        return oc * 128, 128
    if oc == 32:
        return 4096, 128
    if oc == 33:
        return 4224, 128
    if oc == 34:
        return 4352, 64
    return GBASEF + (oc - 35) * 128, 128


def build_fused():
    nc = bass.Bass("TRN2", target_bir_lowering=False)
    xb = nc.dram_tensor("xb", [SEQ, D], F32, kind="ExternalInput").ap()
    wq = nc.dram_tensor("wq", [NHF, DH, D], F32, kind="ExternalInput").ap()
    wk = nc.dram_tensor("wk", [NHF, DH, D], F32, kind="ExternalInput").ap()
    wv = nc.dram_tensor("wv", [NHF, DH, D], F32, kind="ExternalInput").ap()
    tril = nc.dram_tensor("tril", [128, 128], F32, kind="ExternalInput").ap()
    negtril = nc.dram_tensor("negtril", [128, 128], F32, kind="ExternalInput").ap()
    identbf = nc.dram_tensor("identbf", [128, 128], BF16, kind="ExternalInput").ap()
    wo = nc.dram_tensor("wo", [D, D], F32, kind="ExternalInput").ap()
    fw = nc.dram_tensor("fw", [D, D], F32, kind="ExternalInput").ap()
    fb = nc.dram_tensor("fb", [1, D], F32, kind="ExternalInput").ap()
    ones1 = nc.dram_tensor("ones1", [1, 128], F32, kind="ExternalInput").ap()
    inw = nc.dram_tensor("inw", [D, NOUTF], F32, kind="ExternalInput").ap()
    convw = nc.dram_tensor("convw", [4352, 4], F32, kind="ExternalInput").ap()
    convb = nc.dram_tensor("convb", [4352, 1], F32, kind="ExternalInput").ap()
    dtb = nc.dram_tensor("dtb", [HLF, 1], F32, kind="ExternalInput").ap()
    alog = nc.dram_tensor("alog", [HLF, 1], F32, kind="ExternalInput").ap()
    dpar = nc.dram_tensor("dpar", [HLF, 1], F32, kind="ExternalInput").ap()
    sel = nc.dram_tensor("sel", [HLF, HPF], F32, kind="ExternalInput").ap()
    opw = nc.dram_tensor("opw", [HPF, D], F32, kind="ExternalInput").ap()
    nw = nc.dram_tensor("nw", [128, 32], F32, kind="ExternalInput").ap()
    gwT = nc.dram_tensor("gwT", [D, INTER], F32, kind="ExternalInput").ap()
    uwT = nc.dram_tensor("uwT", [D, INTER], F32, kind="ExternalInput").ap()
    dwT = nc.dram_tensor("dwT", [INTER, D], F32, kind="ExternalInput").ap()
    gs = nc.dram_tensor("gs", [128, NIC], F32, kind="ExternalInput").ap()
    us = nc.dram_tensor("us", [128, NIC], F32, kind="ExternalInput").ap()
    dsr = nc.dram_tensor("dsr", [1, D], F32, kind="ExternalInput").ap()
    ones4k = nc.dram_tensor("ones4k", [1, SEQ], F32, kind="ExternalInput").ap()
    out_f = nc.dram_tensor("out_f", [SEQ, D], F32, kind="ExternalOutput").ap()
    xnT_d = nc.dram_tensor("xnT_d", [D, SEQ], BF16, kind="Internal").ap()
    zT_d = nc.dram_tensor("zT_d", [D, SEQ], BF16, kind="Internal").ap()
    x1_d = nc.dram_tensor("x1_d", [SEQ, D], F32, kind="Internal").ap()
    xn1T_d = nc.dram_tensor("xn1T_d", [D, SEQ], BF16, kind="Internal").ap()
    inwbf_d = nc.dram_tensor("inwbf_d", [D, NOUTF], BF16, kind="Internal").ap()
    projT_d = nc.dram_tensor("projT_d", [NOUTF, SEQ], BF16, kind="Internal").ap()
    hbcs_d = nc.dram_tensor("hbcs_d", [4096, SEQ], BF16, kind="Internal").ap()
    LH0_d = nc.dram_tensor("LH0_d", [NCH, HLF * 128], F32, kind="Internal").ap()
    RH1_d = nc.dram_tensor("RH1_d", [NCH, HLF * 128], F32, kind="Internal").ap()
    hT_d = nc.dram_tensor("hT_d", [HPF, SEQ], BF16, kind="Internal").ap()
    gwT_d = nc.dram_tensor("gwT_d", [D, INTER], BF16, kind="Internal").ap()
    uwT_d = nc.dram_tensor("uwT_d", [D, INTER], BF16, kind="Internal").ap()
    dwT_d = nc.dram_tensor("dwT_d", [INTER, D], BF16, kind="Internal").ap()
    opn_d = nc.dram_tensor("opn_d", [HPF, D], BF16, kind="Internal").ap()
    xacc_d = nc.dram_tensor("xacc_d", [SEQ, D], F32, kind="Internal").ap()

    with ExitStack() as octx:
        B = Builder(nc, octx)
        TEN = B.sem("TEN")
        VEC = B.sem("VEC")
        SCA = B.sem("SCA")
        GPS = B.sem("GPS")
        B.set_engine_sems(vector=VEC, scalar=SCA, gpsimd=GPS)

        def _ph_att():
            with B.scope():
                LD_C = B.sem("aLD_C")
                LD_X = [B.sem(f"aLD_X{i}") for i in range(2)]
                LD_W = B.sem("aLD_W")
                LD_XNB = [B.sem(f"aLD_XNB{i}") for i in range(2)]
                ST_XNT = [B.sem(f"aST_XNT{i}") for i in range(2)]
                ST_Z = B.sem("aST_Z")

                xt = [B.sb(f"xt{i}", [128, D]) for i in range(2)]
                xnbf = [B.sb(f"xnbf{i}", [128, D], BF16) for i in range(2)]
                stg = [B.sb(f"stg{i}", [128, D], BF16) for i in range(2)]
                red = B.sb("red", [128, 8])
                epsT = B.sb("epsT", [128, 1])
                trilS = B.sb("trilS", [128, 128])
                ngtS = B.sb("ngtS", [128, 128])
                idbS = B.sb("idbS", [128, 128], BF16)
                wbf = B.sb("wbf", [128, D], BF16)
                wT = {n: B.sb(f"wT_{n}", [128, D], BF16) for n in "qkv"}
                xnb = [B.sb(f"xnb{i}", [128, 16 * 512], BF16) for i in range(2)]
                QT = B.sb("QT", [128, SEQ], BF16)
                KT = B.sb("KT", [128, SEQ], BF16)
                VT = B.sb("VT", [128, SEQ], BF16)
                Vtm = B.sb("Vtm", [128, SEQ], BF16)
                S_sb = [B.sb(f"S_sb{i}", [128, SEQ]) for i in range(2)]
                sqs = S_sb[0][:, 0:D]
                w32 = S_sb[1][:, 0:D]
                P_sb = [B.sb(f"P_sb{i}", [128, SEQ], BF16) for i in range(2)]
                PT_sb = [B.sb(f"PT_sb{i}", [128, SEQ], BF16) for i in range(2)]
                mx = [B.sb(f"mx{i}", [128, 2]) for i in range(2)]
                sump = [B.sb(f"sump{i}", [128, 2]) for i in range(2)]
                zT_sb = B.sb("zT_sb", [128, SEQ], BF16)
                PS = [B.ps(f"aPS{i}", [128, 512]) for i in range(6)]
                PB = [B.ps(f"aPB{i}", [128, 512], BF16) for i in range(2)]

                B.on("vector", lambda e: e.memset(epsT[:], 1e-5), inc=(VEC, 1))
                B.on("sync", lambda e: e.dma_start(out=trilS[:], in_=tril), inc=(LD_C, 16))
                B.on("sync", lambda e: e.dma_start(out=ngtS[:], in_=negtril), inc=(LD_C, 16))
                B.on("sync", lambda e: e.dma_start(out=idbS[:], in_=identbf), inc=(LD_C, 16))

                xn_free = [0, 0]
                stg_free = [0, 0]
                xt_free = [0, 0]
                xt_free_s = [0, 0]
                for tt in range(SEQ // 128):
                    p = tt % 2
                    B.on("sync", lambda e, p=p, tt=tt: e.dma_start(
                        out=xt[p][:], in_=xb[tt * 128:(tt + 1) * 128, :]),
                        inc=(LD_X[p], 16), waits=[(VEC, xt_free[p]), (SCA, xt_free_s[p])])
                    ld_here = LD_X[p].n
                    B.on("vector", lambda e, p=p: e.reduce_sum(red[:, 0:1], xt[p][:], AX),
                         inc=(VEC, 1), waits=[(LD_X[p], ld_here)])
                    B.on("scalar", lambda e, p=p: e.activation(
                        sqs, xt[p][:], AF.Square, accum_out=red[:, 2:3]),
                        inc=(SCA, 1), waits=[(LD_X[p], ld_here), (VEC, None)])
                    B.on("vector", lambda e: e.tensor_scalar(
                        red[:, 1:2], red[:, 0:1], 1.0 / D, None, ALU.mult), inc=(VEC, 1))
                    B.on("vector", lambda e: e.tensor_mul(red[:, 3:4], red[:, 1:2],
                                                          red[:, 1:2]), inc=(VEC, 1))
                    B.on("vector", lambda e: e.tensor_scalar(
                        red[:, 4:5], red[:, 2:3], 1.0 / D, None, ALU.mult),
                        inc=(VEC, 1), waits=[(SCA, None)])
                    B.on("vector", lambda e: e.tensor_sub(red[:, 4:5], red[:, 4:5],
                                                          red[:, 3:4]), inc=(VEC, 1))
                    B.on("scalar", lambda e: e.activation(red[:, 5:6], red[:, 4:5],
                                                          AF.Ln, bias=epsT[:]),
                         inc=(SCA, 1), waits=[(VEC, None)])
                    B.on("scalar", lambda e: e.activation(red[:, 6:7], red[:, 5:6],
                                                          AF.Exp, scale=-0.5),
                         inc=(SCA, 1))
                    B.on("vector", lambda e, p=p: e.tensor_scalar(
                        xnbf[p][:], xt[p][:], red[:, 1:2], red[:, 6:7],
                        ALU.subtract, ALU.mult),
                        inc=(VEC, 1), waits=[(SCA, None), (TEN, xn_free[p])])
                    xt_free[p] = VEC.n
                    xt_free_s[p] = SCA.n
                    for g in range(4):
                        gp = g % 2
                        for j in range(4):
                            dc = g * 4 + j
                            B.on("tensor", lambda e, p=p, gp=gp, j=j, dc=dc: e.transpose(
                                PB[gp][:, j * 128:(j + 1) * 128],
                                xnbf[p][:, dc * 128:(dc + 1) * 128], idbS[:]),
                                inc=(TEN, 1),
                                waits=[(VEC, None), (LD_C, 48)] if j == 0 else [])
                        B.on("vector", lambda e, g=g, gp=gp, p=p: e.tensor_copy(
                            stg[p][:, g * 512:(g + 1) * 512], PB[gp][:]),
                            inc=(VEC, 1),
                            waits=([(TEN, None), (ST_XNT[p], stg_free[p])] if g == 0
                                   else [(TEN, None)]))
                    dst = xnT_d[:, tt * 128:(tt + 1) * 128].rearrange(
                        "(dc r) c -> r dc c", r=128)
                    B.on("gpsimd", lambda e, p=p, dst=dst: e.dma_start(
                        out=dst, in_=stg[p][:].rearrange("r (dc c) -> r dc c", c=128)),
                        inc=(ST_XNT[p], 16), waits=[(VEC, None)])
                    stg_free[p] = ST_XNT[p].n
                    xn_free[p] = TEN.n

                nqt = SEQ // 128
                for h in range(NHF):
                    for wi, (nm, src_) in enumerate((("q", wq), ("k", wk), ("v", wv))):
                        B.on("sync", lambda e, src_=src_, h=h: e.dma_start(
                            out=w32, in_=src_[h]), inc=(LD_W, 16), waits=[(VEC, None)])
                        B.on("vector", lambda e: e.tensor_copy(wbf[:], w32),
                             inc=(VEC, 1), waits=[(LD_W, None), (TEN, None)])
                        for g in range(4):
                            gp = g % 2
                            for j in range(4):
                                dc = g * 4 + j
                                B.on("tensor", lambda e, gp=gp, j=j, dc=dc: e.transpose(
                                    PB[gp][:, j * 128:(j + 1) * 128],
                                    wbf[:, dc * 128:(dc + 1) * 128], idbS[:]),
                                    inc=(TEN, 1), waits=[(VEC, None)] if j == 0 else [])
                            B.on("vector", lambda e, nm=nm, g=g, gp=gp: e.tensor_copy(
                                wT[nm][:, g * 512:(g + 1) * 512], PB[gp][:]),
                                inc=(VEC, 1), waits=[(TEN, None)])
                    xnb_free = [0, 0]
                    for tau in range(SEQ // 512):
                        p = tau % 2
                        for dc in range(16):
                            B.on("sync", lambda e, p=p, dc=dc, tau=tau: e.dma_start(
                                out=xnb[p][:, dc * 512:(dc + 1) * 512],
                                in_=xnT_d[dc * 128:(dc + 1) * 128,
                                          tau * 512:(tau + 1) * 512]),
                                inc=(LD_XNB[p], 16),
                                waits=([(ST_XNT[0], None), (ST_XNT[1], None),
                                        (TEN, xnb_free[p])] if dc == 0 else []))
                        ld_here = LD_XNB[p].n
                        for wi, nm in enumerate("qkv"):
                            bank = 3 * p + wi
                            for dc in range(16):
                                B.on("tensor", lambda e, bank=bank, nm=nm, dc=dc, p=p:
                                     e.matmul(
                                         PS[bank][:], wT[nm][:, dc * 128:(dc + 1) * 128],
                                         xnb[p][:, dc * 512:(dc + 1) * 512],
                                         start=(dc == 0), stop=(dc == 15)),
                                     inc=(TEN, 1),
                                     waits=([(LD_XNB[p], ld_here), (VEC, None)]
                                            if dc == 0 else []))
                            dstT = {"q": QT, "k": KT, "v": VT}[nm]
                            B.on("vector", lambda e, dstT=dstT, tau=tau, bank=bank:
                                 e.tensor_copy(dstT[:, tau * 512:(tau + 1) * 512],
                                               PS[bank][:]),
                                 inc=(VEC, 1), waits=[(TEN, None)])
                        xnb_free[p] = TEN.n
                    for g in range(SEQ // 512):
                        gp = g % 2
                        for j in range(4):
                            kb = g * 4 + j
                            B.on("tensor", lambda e, gp=gp, j=j, kb=kb: e.transpose(
                                PB[gp][:, j * 128:(j + 1) * 128],
                                VT[:, kb * 128:(kb + 1) * 128], idbS[:]),
                                inc=(TEN, 1), waits=[(VEC, None)] if j == 0 else [])
                        B.on("vector", lambda e, g=g, gp=gp: e.tensor_copy(
                            Vtm[:, g * 512:(g + 1) * 512], PB[gp][:]),
                            inc=(VEC, 1), waits=[(TEN, None)])
                    s_free = [0, 0]
                    p_free = [0, 0]
                    z_free = [0, 0]
                    for qi in range(nqt):
                        qp = qi % 2
                        nk = qi + 1
                        nb = (nk + 3) // 4
                        for kb in range(nb):
                            bank = kb % 2
                            B.on("tensor", lambda e, bank=bank, qi=qi, kb=kb: e.matmul(
                                PS[bank][:], QT[:, qi * 128:(qi + 1) * 128],
                                KT[:, kb * 512:(kb + 1) * 512], start=True, stop=True),
                                inc=(TEN, 1),
                                waits=([(VEC, s_free[qp]), (SCA, None)] if kb == 0
                                       else [(SCA, None)]))
                            B.on("scalar", lambda e, qp=qp, kb=kb, bank=bank: e.mul(
                                S_sb[qp][:, kb * 512:(kb + 1) * 512], PS[bank][:],
                                INV_SQRT_DH),
                                inc=(SCA, 1), waits=[(TEN, None)])
                        B.on("vector", lambda e, qp=qp, qi=qi: e.tensor_mul(
                            S_sb[qp][:, qi * 128:(qi + 1) * 128],
                            S_sb[qp][:, qi * 128:(qi + 1) * 128], trilS[:]),
                            inc=(VEC, 1), waits=[(SCA, None)])
                        B.on("vector", lambda e, qp=qp, qi=qi: e.tensor_add(
                            S_sb[qp][:, qi * 128:(qi + 1) * 128],
                            S_sb[qp][:, qi * 128:(qi + 1) * 128], ngtS[:]),
                            inc=(VEC, 1))
                        if nk % 4 != 0:
                            B.on("vector", lambda e, qp=qp, nk=nk, nb=nb: e.memset(
                                S_sb[qp][:, nk * 128:nb * 512], NEGBIG), inc=(VEC, 1))
                        B.on("vector", lambda e, qp=qp, nb=nb: e.reduce_max(
                            mx[qp][:, 0:1], S_sb[qp][:, 0:nb * 512], AX, negate=True),
                            inc=(VEC, 1))
                        B.on("scalar", lambda e, qp=qp, nb=nb: e.activation(
                            P_sb[qp][:, 0:nb * 512], S_sb[qp][:, 0:nb * 512], AF.Exp,
                            bias=mx[qp][:, 0:1], accum_out=sump[qp][:, 0:1]),
                            inc=(SCA, 1), waits=[(VEC, None), (TEN, p_free[qp])])
                        s_free[qp] = SCA.n
                        B.on("vector", lambda e, qp=qp: e.reciprocal(
                            sump[qp][:, 1:2], sump[qp][:, 0:1]),
                            inc=(VEC, 1), waits=[(SCA, None)])
                        B.on("vector", lambda e, qp=qp, nb=nb: e.tensor_scalar(
                            P_sb[qp][:, 0:nb * 512], P_sb[qp][:, 0:nb * 512],
                            sump[qp][:, 1:2], None, ALU.mult),
                            inc=(VEC, 1))
                        for g in range(nb):
                            gp2 = g % 2
                            jmax = min(4, nk - g * 4)
                            for j in range(jmax):
                                kb = g * 4 + j
                                B.on("tensor", lambda e, gp2=gp2, j=j, kb=kb, qp=qp:
                                     e.transpose(
                                         PB[gp2][:, j * 128:(j + 1) * 128],
                                         P_sb[qp][:, kb * 128:(kb + 1) * 128], idbS[:]),
                                     inc=(TEN, 1),
                                     waits=([(VEC, None)] if j == 0 else []))
                            B.on("vector", lambda e, g=g, gp2=gp2, qp=qp, jmax=jmax:
                                 e.tensor_copy(
                                     PT_sb[qp][:, g * 512:g * 512 + jmax * 128],
                                     PB[gp2][:, 0:jmax * 128]),
                                 inc=(VEC, 1), waits=[(TEN, None)])
                        p_free[qp] = TEN.n
                        zbank = 2 + qp
                        for kb in range(nk):
                            B.on("tensor", lambda e, zbank=zbank, kb=kb, qp=qp, nk=nk:
                                 e.matmul(
                                     PS[zbank][:, 0:128], Vtm[:, kb * 128:(kb + 1) * 128],
                                     PT_sb[qp][:, kb * 128:(kb + 1) * 128],
                                     start=(kb == 0), stop=(kb == nk - 1)),
                                 inc=(TEN, 1),
                                 waits=([(VEC, None), (VEC, z_free[qp])] if kb == 0
                                        else []))
                        B.on("vector", lambda e, qp=qp, qi=qi, zbank=zbank: e.tensor_copy(
                            zT_sb[:, qi * 128:(qi + 1) * 128], PS[zbank][:, 0:128]),
                            inc=(VEC, 1), waits=[(TEN, None)])
                        z_free[qp] = VEC.n
                    B.on("gpsimd", lambda e, h=h: e.dma_start(
                        out=zT_d[h * DH:(h + 1) * DH, :], in_=zT_sb[:]),
                        inc=(ST_Z, 16), waits=[(VEC, None)])
                    B.waitq("vector", [(ST_Z, None)])
                B.waitq("gpsimd", [(ST_Z, None), (ST_XNT[0], None), (ST_XNT[1], None)])
        _ph_att()
        B.barrier()

        def _dummy_out():
            with B.scope():
                DZ = B.sem("dzS")
                zt_ = B.sb("dummyz", [128, D])
                B.on("vector", lambda e: e.memset(zt_[:], 0.0), inc=(VEC, 1))
                for tt in range(SEQ // 128):
                    B.on("gpsimd", lambda e, tt=tt: e.dma_start(
                        out=out_f[tt * 128:(tt + 1) * 128, :], in_=zt_[:]),
                        inc=(DZ, 16), waits=[(VEC, None)])
                B.waitq("gpsimd", [(DZ, None)])

        if FUSED_PHASES < 2:
            _dummy_out()
            B.emit()
            return nc

        def _ph_p0():
            with B.scope():
                LD_C = B.sem("bLD_C")
                LD_W = B.sem("bLD_W")
                LD_Z = [B.sem(f"bLD_Z{i}") for i in range(2)]
                LD_XN = [B.sem(f"bLD_XN{i}") for i in range(2)]
                LD_X = [B.sem(f"bLD_X{i}") for i in range(2)]
                ST_X1 = [B.sem(f"bST_X1{i}") for i in range(2)]
                ST_N = [B.sem(f"bST_N{i}") for i in range(2)]

                idbS = B.sb("idbS", [128, 128], BF16)
                ones1S = B.sb("ones1S", [1, 128])
                fbS = B.sb("fbS", [1, D])
                epsT = B.sb("epsT", [128, 1])
                red = B.sb("red", [128, 8])
                wbf = B.sb("wbf", [128, D], BF16)
                alph = B.sb("alph", [128, 2])
                wot = B.sb("wot", [128, 16 * D], BF16)
                bwt = B.sb("bwt", [128, 16 * D], BF16)
                biasb = B.sb("biasb", [128, D])
                zt = [B.sb(f"zt{i}", [128, D], BF16) for i in range(2)]
                xnt = [B.sb(f"xnt{i}", [128, D], BF16) for i in range(2)]
                xt = [B.sb(f"xt{i}", [128, D]) for i in range(2)]
                xacc = [B.sb(f"xacc{i}", [128, D]) for i in range(2)]
                sqs = B.sb("sqs", [128, D])
                w32 = xacc[0][:]
                sgn = sqs[:]
                xnbf = [zt[i][:] for i in range(2)]
                stg = [xnt[i][:] for i in range(2)]
                PS = [B.ps(f"bPS{i}", [128, 512]) for i in range(6)]
                PB = [B.ps(f"bPB{i}", [128, 512], BF16) for i in range(2)]

                B.on("vector", lambda e: e.memset(epsT[:], 1e-5), inc=(VEC, 1))
                B.on("sync", lambda e: e.dma_start(out=idbS[:], in_=identbf), inc=(LD_C, 16))
                B.on("sync", lambda e: e.dma_start(out=ones1S[:], in_=ones1), inc=(LD_C, 16))
                B.on("sync", lambda e: e.dma_start(out=fbS[:], in_=fb), inc=(LD_C, 16))
                for db in range(4):
                    B.on("tensor", lambda e, db=db: e.matmul(
                        PS[db][:], ones1S[:], fbS[:, db * 512:(db + 1) * 512],
                        start=True, stop=True), inc=(TEN, 1), waits=[(LD_C, 48)])
                    B.on("vector", lambda e, db=db: e.tensor_copy(
                        biasb[:, db * 512:(db + 1) * 512], PS[db][:]),
                        inc=(VEC, 1), waits=[(TEN, None)])

                def prep_w0(src_, dstT, binarize):
                    for rc in range(16):
                        B.on("sync", lambda e, src_=src_, rc=rc: e.dma_start(
                            out=w32, in_=src_[rc * 128:(rc + 1) * 128, :]),
                            inc=(LD_W, 16), waits=[(VEC, None), (SCA, None)])
                        if binarize:
                            B.on("vector", lambda e: e.reduce_sum(
                                alph[:, 0:1], w32, AX, apply_absolute_value=True),
                                inc=(VEC, 1), waits=[(LD_W, None)])
                            B.on("vector", lambda e: e.tensor_scalar(
                                alph[:, 1:2], alph[:, 0:1], 1.0 / D, None, ALU.mult),
                                inc=(VEC, 1))
                            B.on("scalar", lambda e: e.sign(sgn, w32),
                                 inc=(SCA, 1), waits=[(LD_W, None)])
                            B.on("vector", lambda e: e.tensor_scalar(
                                wbf[:], sgn, alph[:, 1:2], None, ALU.mult),
                                inc=(VEC, 1), waits=[(SCA, None), (TEN, None)])
                        else:
                            B.on("vector", lambda e: e.tensor_copy(wbf[:], w32),
                                 inc=(VEC, 1), waits=[(LD_W, None), (TEN, None)])
                        for g in range(4):
                            gp = g % 2
                            for j in range(4):
                                cc = g * 4 + j
                                B.on("tensor", lambda e, gp=gp, j=j, cc=cc: e.transpose(
                                    PB[gp][:, j * 128:(j + 1) * 128],
                                    wbf[:, cc * 128:(cc + 1) * 128], idbS[:]),
                                    inc=(TEN, 1), waits=[(VEC, None)] if j == 0 else [])
                            B.on("vector", lambda e, g=g, gp=gp, rc=rc, dstT=dstT:
                                 e.tensor_copy(
                                     dstT[:].rearrange("r (cc d) -> r cc d", d=D)
                                     [:, g * 4:(g + 1) * 4, rc * 128:(rc + 1) * 128],
                                     PB[gp][:].rearrange("r (cc d) -> r cc d", d=128)),
                                 inc=(VEC, 1), waits=[(TEN, None)])

                prep_w0(wo, wot, False)
                prep_w0(fw, bwt, True)

                zt_free = [0, 0]
                xnt_free = [0, 0]
                xt_free = [0, 0]
                xn_free = [0, 0]
                stg_free = [0, 0]
                x1_free = [0, 0]
                xacc_free_s = [0, 0]
                ps_free = VEC.n
                for tt in range(SEQ // 128):
                    p = tt % 2
                    B.on("sync", lambda e, p=p, tt=tt: e.dma_start(
                        out=zt[p][:].rearrange("r (fc c) -> r fc c", c=128),
                        in_=zT_d[:, tt * 128:(tt + 1) * 128].rearrange(
                            "(fc r) c -> r fc c", r=128)),
                        inc=(LD_Z[p], 16), waits=[(TEN, zt_free[p])])
                    ldz = LD_Z[p].n
                    B.on("sync", lambda e, p=p, tt=tt: e.dma_start(
                        out=xnt[p][:].rearrange("r (fc c) -> r fc c", c=128),
                        in_=xnT_d[:, tt * 128:(tt + 1) * 128].rearrange(
                            "(fc r) c -> r fc c", r=128)),
                        inc=(LD_XN[p], 16), waits=[(TEN, xnt_free[p]),
                                                   (ST_N[p], stg_free[p])])
                    ldxn = LD_XN[p].n
                    B.on("sync", lambda e, p=p, tt=tt: e.dma_start(
                        out=xt[p][:], in_=xb[tt * 128:(tt + 1) * 128, :]),
                        inc=(LD_X[p], 16), waits=[(VEC, xt_free[p])])
                    ldx = LD_X[p].n
                    for fc in range(16):
                        for db in range(4):
                            B.on("tensor", lambda e, p=p, fc=fc, db=db: e.matmul(
                                PS[db][:], zt[p][:, fc * 128:(fc + 1) * 128],
                                wot[:, fc * D + db * 512: fc * D + (db + 1) * 512],
                                start=(fc == 0), stop=(fc == 15)),
                                inc=(TEN, 1),
                                waits=([(LD_Z[p], ldz), (VEC, ps_free)]
                                       if (fc == 0 and db == 0) else []))
                    ten_attn = TEN.n
                    for db in range(4):
                        B.on("vector", lambda e, p=p, db=db: e.scalar_tensor_tensor(
                            xacc[p][:, db * 512:(db + 1) * 512], PS[db][:], 1.0,
                            xt[p][:, db * 512:(db + 1) * 512], ALU.mult, ALU.add),
                            inc=(VEC, 1),
                            waits=([(TEN, ten_attn), (LD_X[p], ldx),
                                    (SCA, xacc_free_s[p]), (ST_X1[p], x1_free[p])]
                                   if db == 0 else []))
                    vec_attn = VEC.n
                    for ic in range(16):
                        for db in range(4):
                            B.on("tensor", lambda e, p=p, ic=ic, db=db: e.matmul(
                                PS[db][:], xnt[p][:, ic * 128:(ic + 1) * 128],
                                bwt[:, ic * D + db * 512: ic * D + (db + 1) * 512],
                                start=(ic == 0), stop=(ic == 15)),
                                inc=(TEN, 1),
                                waits=([(LD_XN[p], ldxn), (VEC, vec_attn)]
                                       if (ic == 0 and db == 0) else []))
                    zt_free[p] = TEN.n
                    xnt_free[p] = TEN.n
                    for db in range(4):
                        B.on("vector", lambda e, p=p, db=db: e.scalar_tensor_tensor(
                            xacc[p][:, db * 512:(db + 1) * 512], PS[db][:], 1.0,
                            xacc[p][:, db * 512:(db + 1) * 512], ALU.mult, ALU.add),
                            inc=(VEC, 1), waits=([(TEN, None)] if db == 0 else []))
                    ps_free = VEC.n
                    B.on("vector", lambda e, p=p: e.tensor_add(xacc[p][:], xacc[p][:],
                                                               biasb[:]), inc=(VEC, 1))
                    xt_free[p] = VEC.n
                    B.on("gpsimd", lambda e, p=p, tt=tt: e.dma_start(
                        out=x1_d[tt * 128:(tt + 1) * 128, :], in_=xacc[p][:]),
                        inc=(ST_X1[p], 16), waits=[(VEC, None)])
                    x1_free[p] = ST_X1[p].n
                    B.on("vector", lambda e, p=p: e.reduce_sum(red[:, 0:1], xacc[p][:],
                                                               AX), inc=(VEC, 1))
                    B.on("scalar", lambda e, p=p: e.activation(
                        sqs[:], xacc[p][:], AF.Square, accum_out=red[:, 2:3]),
                        inc=(SCA, 1), waits=[(VEC, None)])
                    B.on("vector", lambda e: e.tensor_scalar(
                        red[:, 1:2], red[:, 0:1], 1.0 / D, None, ALU.mult), inc=(VEC, 1))
                    B.on("vector", lambda e: e.tensor_mul(red[:, 3:4], red[:, 1:2],
                                                          red[:, 1:2]), inc=(VEC, 1))
                    B.on("vector", lambda e: e.tensor_scalar(
                        red[:, 4:5], red[:, 2:3], 1.0 / D, None, ALU.mult),
                        inc=(VEC, 1), waits=[(SCA, None)])
                    B.on("vector", lambda e: e.tensor_sub(red[:, 4:5], red[:, 4:5],
                                                          red[:, 3:4]), inc=(VEC, 1))
                    B.on("scalar", lambda e: e.activation(red[:, 5:6], red[:, 4:5],
                                                          AF.Ln, bias=epsT[:]),
                         inc=(SCA, 1), waits=[(VEC, None)])
                    B.on("scalar", lambda e: e.activation(red[:, 6:7], red[:, 5:6],
                                                          AF.Exp, scale=-0.5),
                         inc=(SCA, 1))
                    B.on("vector", lambda e, p=p: e.tensor_scalar(
                        xnbf[p][:], xacc[p][:], red[:, 1:2], red[:, 6:7],
                        ALU.subtract, ALU.mult),
                        inc=(VEC, 1), waits=[(SCA, None), (TEN, xn_free[p])])
                    xacc_free_s[p] = SCA.n
                    for g in range(4):
                        gp = g % 2
                        for j in range(4):
                            dc = g * 4 + j
                            B.on("tensor", lambda e, p=p, gp=gp, j=j, dc=dc: e.transpose(
                                PB[gp][:, j * 128:(j + 1) * 128],
                                xnbf[p][:, dc * 128:(dc + 1) * 128], idbS[:]),
                                inc=(TEN, 1), waits=[(VEC, None)] if j == 0 else [])
                        B.on("vector", lambda e, g=g, gp=gp, p=p: e.tensor_copy(
                            stg[p][:, g * 512:(g + 1) * 512], PB[gp][:]),
                            inc=(VEC, 1),
                            waits=([(TEN, None), (ST_N[p], stg_free[p])] if g == 0
                                   else [(TEN, None)]))
                    dstn = xn1T_d[:, tt * 128:(tt + 1) * 128].rearrange(
                        "(dc r) c -> r dc c", r=128)
                    B.on("gpsimd", lambda e, p=p, dstn=dstn: e.dma_start(
                        out=dstn, in_=stg[p][:].rearrange("r (dc c) -> r dc c", c=128)),
                        inc=(ST_N[p], 16), waits=[(VEC, None)])
                    stg_free[p] = ST_N[p].n
                    xn_free[p] = TEN.n
                    zt_free[p] = TEN.n
                    xnt_free[p] = TEN.n
                B.waitq("gpsimd", [(ST_X1[0], None), (ST_X1[1], None),
                                   (ST_N[0], None), (ST_N[1], None)])
        _ph_p0()
        B.barrier()
        if FUSED_PHASES < 3:
            _dummy_out()
            B.emit()
            return nc

        def _ph_mam():
            with B.scope():
                LD_C = B.sem("cLD_C")
                idbS = B.sb("idbS", [128, 128], BF16)
                ones1S = B.sb("ones1S", [1, 128])
                ones128w = B.sb("ones128w", [128, 128])
                consts3 = B.sb("consts3", [128, 4])
                SELs = B.sb("SELs", [HLF, HPF])
                Dupb = B.sb("Dupb", [128, HPF], BF16)
                BT_sb = B.sb("BT_sb", [128, SEQ], BF16)
                CT_sb = B.sb("CT_sb", [128, SEQ], BF16)
                DTP = B.sb("DTP", [HLF, 768])
                dtc_bf = B.sb("dtc_bf", [HLF, 128], BF16)
                t1_sb = B.sb("t1_sb", [128, 512])
                t2_sb = B.sb("t2_sb", [128, 512])

                B.on("vector", lambda e: e.memset(ones128w[:], 1.0), inc=(VEC, 1))
                B.on("sync", lambda e: e.dma_start(out=idbS[:], in_=identbf),
                     inc=(LD_C, 16))
                B.on("sync", lambda e: e.dma_start(out=ones1S[:], in_=ones1),
                     inc=(LD_C, 16))
                B.on("sync", lambda e: e.dma_start(out=SELs[:], in_=sel), inc=(LD_C, 16))
                B.on("sync", lambda e: e.dma_start(out=consts3[0:HLF, 0:1], in_=dtb),
                     inc=(LD_C, 16))
                B.on("sync", lambda e: e.dma_start(out=consts3[0:HLF, 1:2], in_=alog),
                     inc=(LD_C, 16))
                B.on("sync", lambda e: e.dma_start(out=consts3[0:HLF, 3:4], in_=dpar),
                     inc=(LD_C, 16))
                ldc_all = LD_C.n
                B.on("scalar", lambda e: e.activation(
                    consts3[0:HLF, 2:3], consts3[0:HLF, 1:2], AF.Exp),
                    inc=(SCA, 1), waits=[(LD_C, ldc_all)])
                B.on("vector", lambda e: e.tensor_scalar(
                    consts3[0:HLF, 2:3], consts3[0:HLF, 2:3], -1.0, None, ALU.mult),
                    inc=(VEC, 1), waits=[(SCA, None)])

                with B.scope():
                    LD_WP = B.sem("cLD_WP")
                    ST_WB = B.sem("cST_WB")
                    HN2 = NOUTF // 2
                    wp32 = B.sb("wp32", [128, HN2])
                    wpbf = B.sb("wpbf", [128, HN2], BF16)
                    PS0 = B.ps("cPS0", [128, 512])
                    PS1 = B.ps("cPS1", [128, 512])
                    for blk in range(HPF // 512):
                        B.on("tensor", lambda e, blk=blk: e.matmul(
                            PS0[0:1, :], consts3[0:HLF, 3:4],
                            SELs[:, blk * 512:(blk + 1) * 512],
                            start=True, stop=True), inc=(TEN, 1),
                            waits=[(LD_C, ldc_all), (VEC, None)])
                        B.on("vector", lambda e, blk=blk: e.tensor_copy(
                            t1_sb[0:1, :], PS0[0:1, :]), inc=(VEC, 1),
                            waits=[(TEN, None)])
                        B.on("tensor", lambda e, blk=blk: e.matmul(
                            PS1[:], ones1S[:], t1_sb[0:1, :], start=True, stop=True),
                            inc=(TEN, 1), waits=[(VEC, None)])
                        B.on("vector", lambda e, blk=blk: e.tensor_copy(
                            Dupb[:, blk * 512:(blk + 1) * 512], PS1[:]),
                            inc=(VEC, 1), waits=[(TEN, None)])
                    for rc in range(16):
                        for hh in range(2):
                            B.on("sync", lambda e, rc=rc, hh=hh: e.dma_start(
                                out=wp32[:], in_=inw[rc * 128:(rc + 1) * 128,
                                                     hh * HN2:(hh + 1) * HN2]),
                                inc=(LD_WP, 16), waits=[(VEC, None)])
                            B.on("vector", lambda e: e.tensor_copy(wpbf[:], wp32[:]),
                                 inc=(VEC, 1), waits=[(LD_WP, None), (ST_WB, None)])
                            B.on("gpsimd", lambda e, rc=rc, hh=hh: e.dma_start(
                                out=inwbf_d[rc * 128:(rc + 1) * 128,
                                            hh * HN2:(hh + 1) * HN2],
                                in_=wpbf[:]),
                                inc=(ST_WB, 16), waits=[(VEC, None)])
                    B.waitq("gpsimd", [(ST_WB, None)])
                B.barrier()

                with B.scope():
                    LD_XIN = B.sem("cLD_XIN")
                    LD_WIN = [B.sem(f"cLD_WIN{i}") for i in range(2)]
                    ST_PJ = [B.sem(f"cST_PJ{i}") for i in range(2)]
                    xin = B.sb("xin", [128, 16 * 512], BF16)
                    winb = [B.sb(f"winb{i}", [128, 16 * 128], BF16) for i in range(2)]
                    stgp = [B.sb(f"stgp{i}", [128, 512], BF16) for i in range(2)]
                    PSA = [B.ps(f"cPSA{i}", [128, 512]) for i in range(2)]
                    win_free = [0, 0]
                    stgp_free = [0, 0]
                    for tb in range(SEQ // 512):
                        for dc in range(16):
                            B.on("sync", lambda e, dc=dc, tb=tb: e.dma_start(
                                out=xin[:, dc * 512:(dc + 1) * 512],
                                in_=xn1T_d[dc * 128:(dc + 1) * 128,
                                           tb * 512:(tb + 1) * 512]),
                                inc=(LD_XIN, 16),
                                waits=([(TEN, None)] if dc == 0 else []))
                        ldxin = LD_XIN.n
                        for oc in range(67):
                            rowbase, rows = _oc_geom_f(oc)
                            p = oc % 2
                            src_ = inwbf_d[:, rowbase:rowbase + rows].rearrange(
                                "(dc r) c -> r dc c", r=128)
                            B.on("sync", lambda e, p=p, src_=src_, rows=rows: e.dma_start(
                                out=winb[p][:].rearrange("r (dc c) -> r dc c", c=128)
                                [:, :, 0:rows], in_=src_),
                                inc=(LD_WIN[p], 16),
                                waits=[(TEN, win_free[p])])
                            ldwin = LD_WIN[p].n
                            for dc in range(16):
                                B.on("tensor", lambda e, p=p, dc=dc, rows=rows: e.matmul(
                                    PSA[p][0:rows, :],
                                    winb[p][:, dc * 128:dc * 128 + rows],
                                    xin[:, dc * 512:(dc + 1) * 512],
                                    start=(dc == 0), stop=(dc == 15)),
                                    inc=(TEN, 1),
                                    waits=([(LD_WIN[p], ldwin), (LD_XIN, ldxin),
                                            (VEC, None)] if dc == 0 else []))
                            win_free[p] = TEN.n
                            B.on("vector", lambda e, p=p, rows=rows: e.tensor_copy(
                                stgp[p][0:rows, :], PSA[p][0:rows, :]),
                                inc=(VEC, 1), waits=[(TEN, None),
                                                     (ST_PJ[p], stgp_free[p])])
                            B.on("gpsimd", lambda e, p=p, rows=rows, rowbase=rowbase,
                                 tb=tb: e.dma_start(
                                     out=projT_d[rowbase:rowbase + rows,
                                                 tb * 512:(tb + 1) * 512],
                                     in_=stgp[p][0:rows, :]),
                                 inc=(ST_PJ[p], 16), waits=[(VEC, None)])
                            stgp_free[p] = ST_PJ[p].n
                    B.waitq("gpsimd", [(ST_PJ[0], None), (ST_PJ[1], None)])
                B.barrier()

                with B.scope():
                    LD_CV = B.sem("cLD_CV")
                    ST_HB = B.sem("cST_HB")
                    convin = B.sb("convin", [128, SEQ + 4], BF16)
                    convacc = B.sb("convacc", [128, SEQ], BF16)
                    convs_bf = B.sb("convs_bf", [128, SEQ], BF16)
                    convwS = B.sb("convwS", [128, 4])
                    convbS = B.sb("convbS", [128, 1])
                    B.on("vector", lambda e: e.memset(convin[:, 0:3], 0.0), inc=(VEC, 1))
                    for rc in range(34):
                        B.on("sync", lambda e, rc=rc: e.dma_start(
                            out=convwS[:, :], in_=convw[rc * 128:(rc + 1) * 128, :]),
                            inc=(LD_CV, 16), waits=[(VEC, None), (SCA, None)])
                        B.on("sync", lambda e, rc=rc: e.dma_start(
                            out=convbS[:, :], in_=convb[rc * 128:(rc + 1) * 128, :]),
                            inc=(LD_CV, 16))
                        B.on("sync", lambda e, rc=rc: e.dma_start(
                            out=convin[:, 3:SEQ + 3],
                            in_=projT_d[rc * 128:(rc + 1) * 128, :]),
                            inc=(LD_CV, 16))
                        ldcv = LD_CV.n
                        B.on("vector", lambda e: e.tensor_scalar(
                            convacc[:], convin[:, 0:SEQ], convwS[:, 0:1], None,
                            ALU.mult),
                            inc=(VEC, 1), waits=[(LD_CV, ldcv), (SCA, None)])
                        for j in range(1, 4):
                            B.on("vector", lambda e, j=j: e.scalar_tensor_tensor(
                                convacc[:], convin[:, j:SEQ + j], convwS[:, j:j + 1],
                                convacc[:], ALU.mult, ALU.add),
                                inc=(VEC, 1))
                        if rc < 32:
                            dst_ = convs_bf[:]
                        elif rc == 32:
                            dst_ = BT_sb[:]
                        else:
                            dst_ = CT_sb[:]
                        B.on("scalar", lambda e, dst_=dst_: e.activation(
                            dst_, convacc[:], AF.Silu, bias=convbS[:, 0:1]),
                            inc=(SCA, 1), waits=[(VEC, None), (ST_HB, None)])
                        if rc < 32:
                            B.on("gpsimd", lambda e, rc=rc: e.dma_start(
                                out=hbcs_d[rc * 128:(rc + 1) * 128, :], in_=convs_bf[:]),
                                inc=(ST_HB, 16), waits=[(SCA, None)])
                    B.waitq("gpsimd", [(ST_HB, None)])
                B.barrier()
                sca_conv = SCA.n

                with B.scope():
                    LD_DT = B.sem("cLD_DT")
                    ST_FL = B.sem("cST_FL")
                    LD_XC = B.sem("cLD_XC")
                    LD_GT = B.sem("cLD_GT")
                    LD_LR = B.sem("cLD_LR")
                    ST_H = [B.sem(f"cST_H{i}") for i in range(2)]
                    LHt = B.sb("LHt", [2, 4096])
                    RHt = B.sb("RHt", [2, 4096])
                    CDR1 = B.sb("CDR1", [1, 512])
                    S32 = B.sb("S32", [128, HPF])
                    Sbf = B.sb("Sbf", [128, HPF], BF16)
                    xch = B.sb("xch", [128, HPF], BF16)
                    x_tm = B.sb("x_tm", [128, HPF], BF16)
                    B_tm = B.sb("B_tm", [128, 128], BF16)
                    w_up = B.sb("w_up", [128, HPF], BF16)
                    E_up = B.sb("E_up", [128, HPF], BF16)
                    xw = B.sb("xw", [128, HPF], BF16)
                    ST_c = B.sb("ST_c", [128, 128], BF16)
                    Dm4 = B.sb("Dm4", [128, 512])
                    G4 = B.sb("G4", [128, 512], BF16)
                    M_sb = B.sb("M_sb", [128, HLF * 128], BF16)
                    yh_tm = B.sb("yh_tm", [128, HPF], BF16)
                    gates = B.sb("gates", [128, HPF], BF16)
                    hstage = [B.sb(f"hstage{i}", [128, HPF], BF16) for i in range(2)]
                    c_u = DTP[:, 0:128]
                    c_r = DTP[:, 128:256]
                    c_dt = DTP[:, 256:384]
                    c_cum = DTP[:, 384:512]
                    c_last = DTP[:, 511:512]
                    c_lnd = DTP[:, 512:640]
                    c_w = DTP[:, 640:768]
                    LD_O4 = B.sem("cLD_O4")
                    B.on("sync", lambda e: e.dma_start(
                        out=LHt[1:2, :], in_=ones4k[:, 0:4096]), inc=(LD_O4, 16))
                    B.on("sync", lambda e: e.dma_start(
                        out=RHt[0:1, :], in_=ones4k[:, 0:4096]), inc=(LD_O4, 16))
                    B.on("vector", lambda e: e.memset(S32[:], 0.0), inc=(VEC, 1))
                    B.waitq("tensor", [(LD_O4, None)])
                    PS = [B.ps(f"dPS{i}", [128, 512]) for i in range(6)]
                    PB = [B.ps(f"dPB{i}", [128, 512], BF16) for i in range(2)]

                    xch_free = 0
                    gt_free = 0
                    gt_free_v = 0
                    hst_free = [0, 0]
                    lhrh_free = 0
                    dtc_free = 0
                    lr_done = 0
                    for c in range(NCH):
                        pc = c % 2
                        B.on("sync", lambda e, c=c: e.dma_start(
                            out=dtc_bf[:], in_=projT_d[4352:4416,
                                                       c * 128:(c + 1) * 128]),
                            inc=(LD_DT, 16), waits=[(VEC, dtc_free)])
                        lddt = LD_DT.n
                        B.on("vector", lambda e: e.tensor_scalar(
                            c_u, dtc_bf[:], consts3[0:HLF, 0:1], None, ALU.add),
                            inc=(VEC, 1), waits=[(LD_DT, lddt), (LD_C, ldc_all),
                                                 (SCA, None)])
                        dtc_free = VEC.n
                        B.on("vector", lambda e: e.tensor_scalar(
                            c_r, c_u, 0.0, None, ALU.max), inc=(VEC, 1))
                        B.on("scalar", lambda e: e.activation(c_u, c_u, AF.Abs),
                             inc=(SCA, 1), waits=[(VEC, None)])
                        B.on("scalar", lambda e: e.activation(c_u, c_u, AF.Exp,
                                                              scale=-1.0), inc=(SCA, 1))
                        B.on("scalar", lambda e: e.activation(
                            c_u, c_u, AF.Ln, bias=ones128w[0:HLF, 0:1]), inc=(SCA, 1))
                        B.on("vector", lambda e: e.tensor_add(c_dt, c_r, c_u),
                             inc=(VEC, 1), waits=[(SCA, None)])
                        B.on("scalar", lambda e: e.activation(c_lnd, c_dt, AF.Ln),
                             inc=(SCA, 1), waits=[(VEC, None)])
                        B.on("vector", lambda e: e.tensor_scalar(
                            c_r, c_dt, consts3[0:HLF, 2:3], None, ALU.mult),
                            inc=(VEC, 1))
                        B.on("vector", lambda e: e.tensor_tensor_scan(
                            c_cum, ones128w[0:HLF, :], c_r, 0.0, ALU.mult, ALU.add),
                            inc=(VEC, 1))
                        B.on("vector", lambda e: e.tensor_sub(c_lnd, c_lnd, c_cum),
                             inc=(VEC, 1), waits=[(SCA, None)])
                        B.on("gpsimd", lambda e, c=c: e.dma_start(
                            out=LH0_d[c:c + 1, :].rearrange("o (h l) -> (o h) l", l=128),
                            in_=c_lnd),
                            inc=(ST_FL, 16), waits=[(VEC, None), (LD_LR, lr_done)])
                        B.on("gpsimd", lambda e, c=c: e.dma_start(
                            out=RH1_d[c:c + 1, :].rearrange("o (h l) -> (o h) l", l=128),
                            in_=c_cum),
                            inc=(ST_FL, 16))
                        B.on("sync", lambda e, c=c: e.dma_start(
                            out=xch[:].rearrange("r (rc l) -> r rc l", l=128),
                            in_=hbcs_d[:, c * 128:(c + 1) * 128].rearrange(
                                "(rc r) l -> r rc l", r=128)),
                            inc=(LD_XC, 16), waits=[(TEN, xch_free), (ST_HB, None)])
                        ldxc = LD_XC.n
                        B.on("sync", lambda e, c=c: e.dma_start(
                            out=gates[:].rearrange("r (rc l) -> r rc l", l=128),
                            in_=projT_d[GBASEF:GBASEF + HPF,
                                        c * 128:(c + 1) * 128].rearrange(
                                "(rc r) l -> r rc l", r=128)),
                            inc=(LD_GT, 16), waits=[(SCA, gt_free),
                                                    (VEC, gt_free_v)])
                        ldgt = LD_GT.n
                        for g in range(8):
                            gp = g % 2
                            for j in range(4):
                                rc = g * 4 + j
                                B.on("tensor", lambda e, gp=gp, j=j, rc=rc: e.transpose(
                                    PB[gp][:, j * 128:(j + 1) * 128],
                                    xch[:, rc * 128:(rc + 1) * 128], idbS[:]),
                                    inc=(TEN, 1),
                                    waits=([(LD_XC, ldxc), (VEC, None)] if j == 0
                                           else []))
                            B.on("vector", lambda e, g=g, gp=gp: e.tensor_copy(
                                x_tm[:, g * 512:(g + 1) * 512], PB[gp][:]),
                                inc=(VEC, 1), waits=[(TEN, None)])
                        B.on("tensor", lambda e, c=c: e.transpose(
                            PB[0][:, 0:128], BT_sb[:, c * 128:(c + 1) * 128], idbS[:]),
                            inc=(TEN, 1), waits=[(VEC, None), (SCA, sca_conv)])
                        B.on("vector", lambda e: e.tensor_copy(B_tm[:], PB[0][:, 0:128]),
                             inc=(VEC, 1), waits=[(TEN, None)])
                        B.on("vector", lambda e: e.tensor_scalar(
                            c_w, c_cum, c_last, None, ALU.subtract), inc=(VEC, 1))
                        B.on("scalar", lambda e: e.activation(c_w, c_w, AF.Exp,
                                                              scale=-1.0),
                             inc=(SCA, 1), waits=[(VEC, None)])
                        B.on("vector", lambda e: e.tensor_mul(c_w, c_w, c_dt),
                             inc=(VEC, 1), waits=[(SCA, None)])
                        for blk in range(8):
                            B.on("tensor", lambda e, blk=blk: e.matmul(
                                PS[2][:], c_w, SELs[:, blk * 512:(blk + 1) * 512],
                                start=True, stop=True),
                                inc=(TEN, 1), waits=[(VEC, None), (SCA, None)])
                            B.on("vector", lambda e, blk=blk: e.tensor_copy(
                                w_up[:, blk * 512:(blk + 1) * 512], PS[2][:]),
                                inc=(VEC, 1), waits=[(TEN, None)])
                        for blk in range(8):
                            B.on("tensor", lambda e, blk=blk: e.matmul(
                                PS[2][:], c_cum, SELs[:, blk * 512:(blk + 1) * 512],
                                start=True, stop=True),
                                inc=(TEN, 1), waits=[(VEC, None), (SCA, None)])
                            B.on("scalar", lambda e, blk=blk: e.activation(
                                E_up[:, blk * 512:(blk + 1) * 512], PS[2][:], AF.Exp),
                                inc=(SCA, 1), waits=[(TEN, None), (VEC, None)])
                        B.on("vector", lambda e: e.tensor_mul(xw[:], x_tm[:], w_up[:]),
                             inc=(VEC, 1), waits=[(SCA, None), (TEN, None)])
                        B.on("vector", lambda e: e.tensor_copy(Sbf[:], S32[:]),
                             inc=(VEC, 1), waits=[(TEN, None)])
                        for blk in range(8):
                            B.on("tensor", lambda e, blk=blk: e.matmul(
                                PS[2][0:1, :], c_last,
                                SELs[:, blk * 512:(blk + 1) * 512],
                                start=True, stop=True),
                                inc=(TEN, 1), waits=[(VEC, None), (SCA, None)])
                            B.on("scalar", lambda e, blk=blk: e.activation(
                                CDR1[:], PS[2][0:1, :], AF.Exp),
                                inc=(SCA, 1), waits=[(TEN, None)])
                            B.on("tensor", lambda e, blk=blk: e.matmul(
                                PS[3][:], ones1S[:], CDR1[:], start=True, stop=True),
                                inc=(TEN, 1), waits=[(SCA, None)])
                            B.on("tensor", lambda e, blk=blk: e.matmul(
                                PS[4][:], B_tm[:], xw[:, blk * 512:(blk + 1) * 512],
                                start=True, stop=True),
                                inc=(TEN, 1))
                            B.on("vector", lambda e, blk=blk: e.tensor_mul(
                                S32[:, blk * 512:(blk + 1) * 512],
                                S32[:, blk * 512:(blk + 1) * 512], PS[3][:]),
                                inc=(VEC, 1), waits=[(TEN, None)])
                            B.on("vector", lambda e, blk=blk: e.tensor_add(
                                S32[:, blk * 512:(blk + 1) * 512],
                                S32[:, blk * 512:(blk + 1) * 512], PS[4][:]),
                                inc=(VEC, 1))
                        B.on("tensor", lambda e, c=c: e.matmul(
                            PS[5][0:128, 0:128], BT_sb[:, c * 128:(c + 1) * 128],
                            CT_sb[:, c * 128:(c + 1) * 128], start=True, stop=True),
                            inc=(TEN, 1), waits=[(VEC, None), (SCA, sca_conv)])
                        B.on("vector", lambda e: e.tensor_copy(ST_c[:], PS[5][:, 0:128]),
                             inc=(VEC, 1), waits=[(TEN, None)])
                        strep = ST_c[:].unsqueeze(1).broadcast_to([128, 4, 128])
                        for dh in range(2):
                            B.on("sync", lambda e, c=c, dh=dh: e.dma_start(
                                out=LHt[0:1, :],
                                in_=LH0_d[c:c + 1, dh * 4096:(dh + 1) * 4096]),
                                inc=(LD_LR, 16), waits=[(TEN, lhrh_free),
                                                        (ST_FL, None)])
                            B.on("sync", lambda e, c=c, dh=dh: e.dma_start(
                                out=RHt[1:2, :],
                                in_=RH1_d[c:c + 1, dh * 4096:(dh + 1) * 4096]),
                                inc=(LD_LR, 16))
                            ldlr = LD_LR.n
                            lr_done = LD_LR.n
                            for hg in range(8):
                                pd = hg % 2
                                for j in range(4):
                                    hl = hg * 4 + j
                                    B.on("tensor", lambda e, pd=pd, j=j, hl=hl: e.matmul(
                                        PS[pd][:, j * 128:(j + 1) * 128],
                                        LHt[:, hl * 128:(hl + 1) * 128],
                                        RHt[:, hl * 128:(hl + 1) * 128],
                                        start=True, stop=True),
                                        inc=(TEN, 1),
                                        waits=([(LD_LR, ldlr), (VEC, None)]
                                               if (hg == 0 and j == 0)
                                               else ([(VEC, None)] if j == 0 else [])))
                                B.on("vector", lambda e, pd=pd: e.tensor_scalar(
                                    Dm4[:], PS[pd][:], DCLAMP, None, ALU.min),
                                    inc=(VEC, 1), waits=[(TEN, None), (SCA, None)])
                                B.on("scalar", lambda e: e.activation(G4[:], Dm4[:],
                                                                      AF.Exp),
                                     inc=(SCA, 1), waits=[(VEC, None)])
                                hgg = dh * 8 + hg
                                B.on("vector", lambda e, hgg=hgg: e.tensor_tensor(
                                    M_sb[:, hgg * 512:(hgg + 1) * 512].rearrange(
                                        "r (j i) -> r j i", i=128),
                                    G4[:].rearrange("r (j i) -> r j i", i=128),
                                    strep, ALU.mult),
                                    inc=(VEC, 1), waits=[(SCA, None), (TEN, None)])
                                B.on("gpsimd", lambda e, hgg=hgg: e.affine_select(
                                    M_sb[:, hgg * 512:(hgg + 1) * 512].rearrange(
                                        "r (j i) -> r j i", i=128),
                                    M_sb[:, hgg * 512:(hgg + 1) * 512].rearrange(
                                        "r (j i) -> r j i", i=128),
                                    [[0, 4], [1, 128]], ALU.is_ge, 0.0,
                                    base=0, channel_multiplier=-1),
                                    inc=(GPS, 1), waits=[(VEC, None)])
                            lhrh_free = TEN.n
                        gps_m = GPS.n
                        for yg in range(8):
                            for j in range(8):
                                hl = yg * 8 + j
                                B.on("tensor", lambda e, j=j, hl=hl: e.matmul(
                                    PS[3][:, j * 64:(j + 1) * 64],
                                    M_sb[:, hl * 128:(hl + 1) * 128],
                                    x_tm[:, hl * 64:(hl + 1) * 64],
                                    start=True, stop=True),
                                    inc=(TEN, 1),
                                    waits=([(GPS, gps_m), (VEC, None)] if j == 0
                                           else []))
                            B.on("tensor", lambda e, c=c, yg=yg: e.matmul(
                                PS[5][:], CT_sb[:, c * 128:(c + 1) * 128],
                                Sbf[:, yg * 512:(yg + 1) * 512], start=True, stop=True),
                                inc=(TEN, 1))
                            B.on("vector", lambda e, yg=yg: e.tensor_mul(
                                t1_sb[:], PS[5][:], E_up[:, yg * 512:(yg + 1) * 512]),
                                inc=(VEC, 1), waits=[(TEN, None)])
                            B.on("vector", lambda e, yg=yg: e.tensor_mul(
                                t2_sb[:], x_tm[:, yg * 512:(yg + 1) * 512],
                                Dupb[:, yg * 512:(yg + 1) * 512]),
                                inc=(VEC, 1))
                            B.on("vector", lambda e, yg=yg: e.scalar_tensor_tensor(
                                yh_tm[:, yg * 512:(yg + 1) * 512], PS[3][:, :], 1.0,
                                t1_sb[:], ALU.mult, ALU.add),
                                inc=(VEC, 1))
                            B.on("vector", lambda e, yg=yg: e.tensor_add(
                                yh_tm[:, yg * 512:(yg + 1) * 512],
                                yh_tm[:, yg * 512:(yg + 1) * 512], t2_sb[:]),
                                inc=(VEC, 1))
                        B.on("scalar", lambda e: e.activation(gates[:], gates[:],
                                                              AF.Silu),
                             inc=(SCA, 1), waits=[(LD_GT, ldgt), (VEC, None)])
                        gt_free = SCA.n
                        for g in range(8):
                            gp = g % 2
                            for j in range(4):
                                rc = g * 4 + j
                                B.on("tensor", lambda e, gp=gp, j=j, rc=rc: e.transpose(
                                    PB[gp][:, j * 128:(j + 1) * 128],
                                    yh_tm[:, rc * 128:(rc + 1) * 128], idbS[:]),
                                    inc=(TEN, 1), waits=([(VEC, None), (SCA, None)]
                                                         if j == 0 else []))
                            B.on("vector", lambda e, g=g, gp=gp, pc=pc: e.tensor_mul(
                                hstage[pc][:, g * 512:(g + 1) * 512], PB[gp][:],
                                gates[:, g * 512:(g + 1) * 512]),
                                inc=(VEC, 1), waits=([(TEN, None),
                                                      (ST_H[pc], hst_free[pc])]
                                                     if g == 0 else [(TEN, None)]))
                        gt_free_v = VEC.n
                        B.on("gpsimd", lambda e, pc=pc, c=c: e.dma_start(
                            out=hT_d[:, c * 128:(c + 1) * 128].rearrange(
                                "(rc r) l -> r rc l", r=128),
                            in_=hstage[pc][:].rearrange("r (rc l) -> r rc l", l=128)),
                            inc=(ST_H[pc], 16), waits=[(VEC, None)])
                        hst_free[pc] = ST_H[pc].n
                        xch_free = TEN.n
                    B.waitq("gpsimd", [(ST_H[0], None), (ST_H[1], None)])
        _ph_mam()
        B.barrier()
        if FUSED_PHASES < 4:
            _dummy_out()
            B.emit()
            return nc

        def _ph_p1():
            with B.scope():
                LD_C = B.sem("eLD_C")
                LD_PW = B.sem("eLD_PW")
                ST_PW = B.sem("eST_PW")
                LD_HT = B.sem("eLD_HT")
                LD_OP = [B.sem(f"eLD_OP{i}") for i in range(2)]
                LD_X1 = [B.sem(f"eLD_X1{i}") for i in range(2)]
                ST_XA = [B.sem(f"eST_XA{i}") for i in range(4)]
                LD_XI = B.sem("eLD_XI")
                LD_GW = [B.sem(f"eLD_GW{i}") for i in range(2)]
                LD_DW = [B.sem(f"eLD_DW{i}") for i in range(2)]
                LD_XA = [B.sem(f"eLD_XA{i}") for i in range(2)]
                ST_O = [B.sem(f"eST_O{i}") for i in range(2)]

                ones1S = B.sb("ones1S", [1, 128])
                onescol = B.sb("onescol", [128, 1])
                epsT = B.sb("epsT", [128, 1])
                nwS = B.sb("nwS", [128, 32])
                gsS = B.sb("gsS", [128, NIC])
                usS = B.sb("usS", [128, NIC])
                dsS = B.sb("dsS", [1, D])
                DS = B.sb("DS", [128, D])
                rstdr = B.sb("rstdr", [1, SEQ])
                RST = B.sb("RST", [128, SEQ], BF16)
                outacc = B.sb("outacc", [128, 4 * D])
                hdng = B.sb("hdng", [128, 11 * 512], BF16)
                pw32 = outacc[:, 0:INTER]
                pwbf = hdng[:, 0:INTER // 2]
                stage = [B.sb(f"stage{i}", [128, D]) for i in range(4)]
                sq = stage[0][:]
                hts = [B.sb(f"hts{i}", [128, D], BF16) for i in range(2)]
                hn = B.sb("hn", [128, 32 * 512], BF16)
                opt = [B.sb(f"opt{i}", [128, 1024], BF16) for i in range(2)]
                x1t = [B.sb(f"x1t{i}", [128, D]) for i in range(2)]
                xin = B.sb("xin", [128, 16 * 512], BF16)
                gwt = [B.sb(f"gwt{i}", [128, 16 * 128], BF16) for i in range(2)]
                dwt = [B.sb(f"dwt{i}", [128, 512], BF16) for i in range(2)]
                sg = B.sb("sg", [128, 512])
                t2 = B.sb("t2", [128, 512])
                xat = [x1t[0][:], x1t[1][:]]
                ot = [stage[0][:], stage[1][:]]
                PS = [B.ps(f"ePS{i}", [128, 512]) for i in range(8)]
                pr_bank = lambda tt, db2: tt * 2 + db2

                B.on("vector", lambda e: e.memset(onescol[:], 1.0), inc=(VEC, 1))
                B.on("vector", lambda e: e.memset(epsT[:], 1e-5), inc=(VEC, 1))
                B.on("sync", lambda e: e.dma_start(out=ones1S[:], in_=ones1),
                     inc=(LD_C, 16))
                B.on("sync", lambda e: e.dma_start(out=nwS[:], in_=nw), inc=(LD_C, 16))
                B.on("sync", lambda e: e.dma_start(out=gsS[:], in_=gs), inc=(LD_C, 16))
                B.on("sync", lambda e: e.dma_start(out=usS[:], in_=us), inc=(LD_C, 16))
                B.on("sync", lambda e: e.dma_start(out=dsS[:], in_=dsr), inc=(LD_C, 16))
                ldc = LD_C.n
                for db in range(4):
                    B.on("tensor", lambda e, db=db: e.matmul(
                        PS[0][:], ones1S[:], dsS[:, db * 512:(db + 1) * 512],
                        start=True, stop=True), inc=(TEN, 1),
                        waits=[(LD_C, ldc), (VEC, None)] if db == 0
                        else [(VEC, None)])
                    B.on("vector", lambda e, db=db: e.tensor_copy(
                        DS[:, db * 512:(db + 1) * 512], PS[0][:]),
                        inc=(VEC, 1), waits=[(TEN, None)])

                def cast_w1(src_, dst_, nrows, width, scale_nw=False):
                    nh = (width + INTER // 2 - 1) // (INTER // 2)
                    hw = width // nh if nh > 1 else width
                    for rc in range(nrows // 128):
                        for hh in range(nh):
                            B.on("sync", lambda e, src_=src_, rc=rc, hh=hh, hw=hw:
                                 e.dma_start(
                                     out=pw32[:, 0:hw],
                                     in_=src_[rc * 128:(rc + 1) * 128,
                                              hh * hw:(hh + 1) * hw]),
                                 inc=(LD_PW, 16), waits=[(VEC, None)])
                            if scale_nw:
                                B.on("vector", lambda e, rc=rc, hw=hw: e.tensor_scalar(
                                    pwbf[:, 0:hw], pw32[:, 0:hw], nwS[:, rc:rc + 1],
                                    None, ALU.mult),
                                    inc=(VEC, 1), waits=[(LD_PW, None), (ST_PW, None),
                                                         (LD_C, ldc)])
                            else:
                                B.on("vector", lambda e, hw=hw: e.tensor_copy(
                                    pwbf[:, 0:hw], pw32[:, 0:hw]),
                                    inc=(VEC, 1), waits=[(LD_PW, None), (ST_PW, None)])
                            B.on("gpsimd", lambda e, dst_=dst_, rc=rc, hh=hh, hw=hw:
                                 e.dma_start(
                                     out=dst_[rc * 128:(rc + 1) * 128,
                                              hh * hw:(hh + 1) * hw],
                                     in_=pwbf[:, 0:hw]),
                                 inc=(ST_PW, 16), waits=[(VEC, None)])

                cast_w1(gwT, gwT_d, D, INTER)
                cast_w1(uwT, uwT_d, D, INTER)
                cast_w1(dwT, dwT_d, INTER, D)
                cast_w1(opw, opn_d, HPF, D, scale_nw=True)

                ht_free = 0
                for rc in range(32):
                    B.on("sync", lambda e, rc=rc: e.dma_start(
                        out=hts[rc % 2][:], in_=hT_d[rc * 128:(rc + 1) * 128, 0:D]),
                        inc=(LD_HT, 16), waits=[(SCA, ht_free)])
                    ldht = LD_HT.n
                    B.on("scalar", lambda e, rc=rc: e.activation(
                        sq, hts[rc % 2][:], AF.Square),
                        inc=(SCA, 1), waits=[(LD_HT, ldht), (VEC, None), (TEN, None)]
                        if rc == 0 else [(LD_HT, ldht)])
                    ht_free = SCA.n
                    for sb_ in range(4):
                        B.on("tensor", lambda e, rc=rc, sb_=sb_: e.matmul(
                            PS[4 + sb_][0:1, :], onescol[:],
                            sq[:, sb_ * 512:(sb_ + 1) * 512],
                            start=(rc == 0), stop=(rc == 31)),
                            inc=(TEN, 1), waits=[(SCA, None)] if sb_ == 0 else [])
                for sb_ in range(4):
                    B.on("vector", lambda e, sb_=sb_: e.tensor_scalar(
                        rstdr[:, sb_ * 512:(sb_ + 1) * 512], PS[4 + sb_][0:1, :],
                        1.0 / HPF, None, ALU.mult),
                        inc=(VEC, 1), waits=[(TEN, None)])
                for rc in range(32):
                    B.on("sync", lambda e, rc=rc: e.dma_start(
                        out=hts[rc % 2][:], in_=hT_d[rc * 128:(rc + 1) * 128, D:SEQ]),
                        inc=(LD_HT, 16), waits=[(SCA, ht_free), (VEC, None)])
                    ldht = LD_HT.n
                    B.on("scalar", lambda e, rc=rc: e.activation(
                        sq, hts[rc % 2][:], AF.Square),
                        inc=(SCA, 1), waits=[(LD_HT, ldht), (VEC, None)]
                        if rc == 0 else [(LD_HT, ldht)])
                    ht_free = SCA.n
                    for sb_ in range(4):
                        B.on("tensor", lambda e, rc=rc, sb_=sb_: e.matmul(
                            PS[4 + sb_][0:1, :], onescol[:],
                            sq[:, sb_ * 512:(sb_ + 1) * 512],
                            start=(rc == 0), stop=(rc == 31)),
                            inc=(TEN, 1), waits=[(SCA, None), (VEC, None)]
                            if sb_ == 0 else [])
                for sb_ in range(4):
                    B.on("vector", lambda e, sb_=sb_: e.tensor_scalar(
                        rstdr[:, D + sb_ * 512:D + (sb_ + 1) * 512],
                        PS[4 + sb_][0:1, :], 1.0 / HPF, None, ALU.mult),
                        inc=(VEC, 1), waits=[(TEN, None)])
                B.on("scalar", lambda e: e.activation(rstdr[:], rstdr[:], AF.Ln,
                                                      bias=epsT[0:1, 0:1]),
                     inc=(SCA, 1), waits=[(VEC, None)])
                B.on("scalar", lambda e: e.activation(rstdr[:], rstdr[:], AF.Exp,
                                                      scale=-0.5), inc=(SCA, 1))
                for db in range(SEQ // 512):
                    B.on("tensor", lambda e, db=db: e.matmul(
                        PS[0][:], ones1S[:], rstdr[:, db * 512:(db + 1) * 512],
                        start=True, stop=True), inc=(TEN, 1),
                        waits=[(SCA, None), (VEC, None)])
                    B.on("vector", lambda e, db=db: e.tensor_copy(
                        RST[:, db * 512:(db + 1) * 512], PS[0][:]),
                        inc=(VEC, 1), waits=[(TEN, None)])

                hn_free = 0
                opt_free = [0, 0]
                x1_free = [0, 0]
                st_free = [0, 0, 0, 0]
                for tb in range(SEQ // 512):
                    for rc in range(32):
                        B.on("sync", lambda e, rc=rc, tb=tb: e.dma_start(
                            out=hts[rc % 2][:, 0:512],
                            in_=hT_d[rc * 128:(rc + 1) * 128,
                                     tb * 512:(tb + 1) * 512]),
                            inc=(LD_HT, 16), waits=[(VEC, None)])
                        ldht = LD_HT.n
                        B.on("vector", lambda e, rc=rc, tb=tb: e.tensor_mul(
                            hn[:, rc * 512:(rc + 1) * 512], hts[rc % 2][:, 0:512],
                            RST[:, tb * 512:(tb + 1) * 512]),
                            inc=(VEC, 1),
                            waits=([(LD_HT, ldht), (TEN, hn_free)] if rc == 0
                                   else [(LD_HT, ldht)]))
                    for dbh in range(2):
                        for rc in range(32):
                            B.on("sync", lambda e, rc=rc, dbh=dbh: e.dma_start(
                                out=opt[rc % 2][:],
                                in_=opn_d[rc * 128:(rc + 1) * 128,
                                          dbh * 1024:(dbh + 1) * 1024]),
                                inc=(LD_OP[rc % 2], 16),
                                waits=[(TEN, opt_free[rc % 2]), (ST_PW, None)])
                            ldop = LD_OP[rc % 2].n
                            for pr in range(8):
                                tt, db2 = pr // 2, pr % 2
                                B.on("tensor", lambda e, rc=rc, tt=tt, db2=db2:
                                     e.matmul(
                                         PS[pr_bank(tt, db2)][:],
                                         hn[:, rc * 512 + tt * 128:
                                            rc * 512 + (tt + 1) * 128],
                                         opt[rc % 2][:, db2 * 512:(db2 + 1) * 512],
                                         start=(rc == 0), stop=(rc == 31)),
                                     inc=(TEN, 1),
                                     waits=([(LD_OP[rc % 2], ldop), (VEC, None)]
                                            if pr == 0 else []))
                            opt_free[rc % 2] = TEN.n
                        for tt in range(4):
                            tp = tt % 2
                            B.on("sync", lambda e, tp=tp, tb=tb, tt=tt: e.dma_start(
                                out=x1t[tp][:],
                                in_=x1_d[tb * 512 + tt * 128:
                                         tb * 512 + (tt + 1) * 128, :]),
                                inc=(LD_X1[tp], 16), waits=[(VEC, x1_free[tp])])
                            ldx1 = LD_X1[tp].n
                            for db2 in range(2):
                                B.on("vector", lambda e, tp=tp, tt=tt, db2=db2,
                                     dbh=dbh: e.scalar_tensor_tensor(
                                         stage[tt][:, dbh * 1024 + db2 * 512:
                                                   dbh * 1024 + (db2 + 1) * 512],
                                         PS[pr_bank(tt, db2)][:], 1.0,
                                         x1t[tp][:, dbh * 1024 + db2 * 512:
                                                 dbh * 1024 + (db2 + 1) * 512],
                                         ALU.mult, ALU.add),
                                     inc=(VEC, 1),
                                     waits=([(TEN, None), (LD_X1[tp], ldx1),
                                             (ST_XA[tt], st_free[tt])]
                                            if db2 == 0 else []))
                            x1_free[tp] = VEC.n
                            if dbh == 1:
                                B.on("gpsimd", lambda e, tt=tt, tb=tb: e.dma_start(
                                    out=xacc_d[tb * 512 + tt * 128:
                                               tb * 512 + (tt + 1) * 128, :],
                                    in_=stage[tt][:]),
                                    inc=(ST_XA[tt], 16), waits=[(VEC, None)])
                                st_free[tt] = ST_XA[tt].n
                    hn_free = TEN.n

                gw_free = [0, 0]
                dw_free = [0, 0]
                xa_free = [0, 0]
                o_free = [0, 0]
                for tb in range(SEQ // 512):
                    for dc in range(16):
                        B.on("sync", lambda e, dc=dc, tb=tb: e.dma_start(
                            out=xin[:, dc * 512:(dc + 1) * 512],
                            in_=xn1T_d[dc * 128:(dc + 1) * 128,
                                       tb * 512:(tb + 1) * 512]),
                            inc=(LD_XI, 16), waits=([(TEN, None)] if dc == 0 else []))
                    ldxi = LD_XI.n
                    B.on("vector", lambda e: e.memset(outacc[:], 0.0),
                         inc=(VEC, 1), waits=[(TEN, None)])
                    for icg in range(4):
                        for ici in range(11):
                            ic = icg * 11 + ici
                            pg = ici % 2
                            B.on("sync", lambda e, pg=pg, ic=ic: e.dma_start(
                                out=gwt[pg][:].rearrange("r (dc c) -> r dc c", c=128),
                                in_=gwT_d[:, ic * 128:(ic + 1) * 128].rearrange(
                                    "(dc r) c -> r dc c", r=128)),
                                inc=(LD_GW[pg], 16),
                                waits=[(TEN, gw_free[pg]), (ST_PW, None)])
                            ldgw = LD_GW[pg].n
                            for dc in range(16):
                                B.on("tensor", lambda e, pg=pg, dc=dc: e.matmul(
                                    PS[4][:], gwt[pg][:, dc * 128:(dc + 1) * 128],
                                    xin[:, dc * 512:(dc + 1) * 512],
                                    start=(dc == 0), stop=(dc == 15)),
                                    inc=(TEN, 1),
                                    waits=([(LD_GW[pg], ldgw), (LD_XI, ldxi),
                                            (VEC, None)] if dc == 0 else []))
                            B.on("sync", lambda e, pg=pg, ic=ic: e.dma_start(
                                out=gwt[pg][:].rearrange("r (dc c) -> r dc c", c=128),
                                in_=uwT_d[:, ic * 128:(ic + 1) * 128].rearrange(
                                    "(dc r) c -> r dc c", r=128)),
                                inc=(LD_GW[pg], 16), waits=[(TEN, None)])
                            ldgw = LD_GW[pg].n
                            for dc in range(16):
                                B.on("tensor", lambda e, pg=pg, dc=dc: e.matmul(
                                    PS[5][:], gwt[pg][:, dc * 128:(dc + 1) * 128],
                                    xin[:, dc * 512:(dc + 1) * 512],
                                    start=(dc == 0), stop=(dc == 15)),
                                    inc=(TEN, 1),
                                    waits=([(LD_GW[pg], ldgw), (VEC, None)]
                                           if dc == 0 else []))
                            gw_free[pg] = TEN.n
                            B.on("vector", lambda e, ic=ic: e.tensor_scalar(
                                sg[:], PS[4][:], gsS[:, ic:ic + 1], None, ALU.mult),
                                inc=(VEC, 1), waits=[(TEN, None)])
                            B.on("scalar", lambda e: e.activation(sg[:], sg[:],
                                                                  AF.Silu),
                                 inc=(SCA, 1), waits=[(VEC, None)])
                            B.on("vector", lambda e, ic=ic: e.tensor_scalar(
                                t2[:], PS[5][:], usS[:, ic:ic + 1], None, ALU.mult),
                                inc=(VEC, 1))
                            B.on("vector", lambda e, ici=ici: e.tensor_mul(
                                hdng[:, ici * 512:(ici + 1) * 512], sg[:], t2[:]),
                                inc=(VEC, 1), waits=[(SCA, None), (TEN, None)])
                        for hb in range(4):
                            for ici in range(11):
                                ic = icg * 11 + ici
                                pd = ici % 2
                                B.on("sync", lambda e, pd=pd, ic=ic, hb=hb: e.dma_start(
                                    out=dwt[pd][:],
                                    in_=dwT_d[ic * 128:(ic + 1) * 128,
                                              hb * 512:(hb + 1) * 512]),
                                    inc=(LD_DW[pd], 16),
                                    waits=[(TEN, dw_free[pd]), (ST_PW, None)])
                                lddw = LD_DW[pd].n
                                for tt in range(4):
                                    B.on("tensor", lambda e, pd=pd, ici=ici, tt=tt:
                                         e.matmul(
                                             PS[tt][:],
                                             hdng[:, ici * 512 + tt * 128:
                                                  ici * 512 + (tt + 1) * 128],
                                             dwt[pd][:],
                                             start=(ici == 0), stop=(ici == 10)),
                                         inc=(TEN, 1),
                                         waits=([(LD_DW[pd], lddw), (VEC, None)]
                                                if tt == 0 else []))
                                dw_free[pd] = TEN.n
                            for tt in range(4):
                                B.on("vector", lambda e, tt=tt, hb=hb:
                                     e.scalar_tensor_tensor(
                                         outacc[:, tt * D + hb * 512:
                                                tt * D + (hb + 1) * 512],
                                         PS[tt][:], 1.0,
                                         outacc[:, tt * D + hb * 512:
                                                tt * D + (hb + 1) * 512],
                                         ALU.mult, ALU.add),
                                     inc=(VEC, 1), waits=([(TEN, None)] if tt == 0
                                                          else []))
                    for tt in range(4):
                        tp = tt % 2
                        B.on("sync", lambda e, tp=tp, tb=tb, tt=tt: e.dma_start(
                            out=xat[tp][:], in_=xacc_d[tb * 512 + tt * 128:
                                                       tb * 512 + (tt + 1) * 128, :]),
                            inc=(LD_XA[tp], 16),
                            waits=[(VEC, xa_free[tp]), (ST_XA[0], None),
                                   (ST_XA[1], None), (ST_XA[2], None),
                                   (ST_XA[3], None)])
                        ldxa = LD_XA[tp].n
                        for db in range(4):
                            B.on("vector", lambda e, tp=tp, tt=tt, db=db: e.tensor_mul(
                                ot[tp][:, db * 512:(db + 1) * 512],
                                outacc[:, tt * D + db * 512: tt * D + (db + 1) * 512],
                                DS[:, db * 512:(db + 1) * 512]),
                                inc=(VEC, 1),
                                waits=([(LD_XA[tp], ldxa), (ST_O[tp], o_free[tp])]
                                       if db == 0 else []))
                        B.on("vector", lambda e, tp=tp: e.tensor_add(
                            ot[tp], ot[tp], xat[tp]), inc=(VEC, 1))
                        xa_free[tp] = VEC.n
                        B.on("gpsimd", lambda e, tp=tp, tb=tb, tt=tt: e.dma_start(
                            out=out_f[tb * 512 + tt * 128: tb * 512 + (tt + 1) * 128,
                                      :],
                            in_=ot[tp][:]),
                            inc=(ST_O[tp], 16), waits=[(VEC, None)])
                        o_free[tp] = ST_O[tp].n
                B.waitq("gpsimd", [(ST_O[0], None), (ST_O[1], None)])

        _ph_p1()
        if FUSED_P1SUB != 4:
            _dummy_out()
        B.emit()
    return nc



_cache = {}


def _get(name, fn):
    if name not in _cache:
        _cache[name] = fn()
    return _cache[name]


def kernel(**inp):
    cst = consts_host()
    x = np.ascontiguousarray(inp["L0_attn_x"])
    W = inp["L1_attn_in_proj_w"]
    inwf = np.ascontiguousarray(np.concatenate([
        W[:, 4096:8192], W[:, 8192:8320], W[:, 8320:8448],
        W[:, 8448:8512], W[:, 0:4096]], axis=1))
    self_sel = np.zeros((HLF, HPF), np.float32)
    for h in range(HLF):
        self_sel[h, h * P_:(h + 1) * P_] = 1.0
    nc = _get("fused", build_fused)
    in_maps = []
    for core in range(8):
        b = core // 2
        in_maps.append(dict(
            xb=np.ascontiguousarray(x[b]),
            wq=np.ascontiguousarray(inp["L0_attn_W_Q"]),
            wk=np.ascontiguousarray(inp["L0_attn_W_K"]),
            wv=np.ascontiguousarray(inp["L0_attn_W_V"]),
            tril=cst["tril"], negtril=cst["negtril"], identbf=cst["identbf"],
            wo=np.ascontiguousarray(inp["L0_attn_W_O"]),
            fw=np.ascontiguousarray(inp["L0_ffn_weight"]),
            fb=np.ascontiguousarray(inp["L0_ffn_bias"].reshape(1, D)),
            ones1=cst["ones1"],
            inw=inwf,
            convw=np.ascontiguousarray(inp["L1_attn_conv_w"]),
            convb=np.ascontiguousarray(inp["L1_attn_conv_b"].reshape(4352, 1)),
            dtb=np.ascontiguousarray(inp["L1_attn_dt_bias"].reshape(HLF, 1)),
            alog=np.ascontiguousarray(inp["L1_attn_A_log"].reshape(HLF, 1)),
            dpar=np.ascontiguousarray(inp["L1_attn_D_param"].reshape(HLF, 1)),
            sel=self_sel,
            opw=np.ascontiguousarray(inp["L1_attn_out_proj_w"]),
            nw=np.ascontiguousarray(inp["L1_attn_norm_w"].reshape(32, 128).T),
            gwT=np.ascontiguousarray(inp["L1_ffn_gate_w"].T),
            uwT=np.ascontiguousarray(inp["L1_ffn_up_w"].T),
            dwT=np.ascontiguousarray(inp["L1_ffn_down_w"].T),
            gs=np.ascontiguousarray(inp["L1_ffn_gate_s"].reshape(NIC, 128).T),
            us=np.ascontiguousarray(inp["L1_ffn_up_s"].reshape(NIC, 128).T),
            dsr=np.ascontiguousarray(inp["L1_ffn_down_s"].reshape(1, D)),
            ones4k=np.ones((1, SEQ), np.float32),
        ))
    res = _launch(nc, in_maps)
    out = np.stack([res[2 * b]["out_f"] for b in range(4)])
    return np.ascontiguousarray(out.astype(np.float32))


def kernel_unfused(**inp):
    cst = consts_host()
    x = np.ascontiguousarray(inp["L0_attn_x"])

    nc1 = _get("attn0", build_attn0)
    in_maps = []
    for core in range(8):
        b, hi = core // 2, core % 2
        in_maps.append(dict(
            xb=np.ascontiguousarray(x[b]),
            wq=np.ascontiguousarray(inp["L0_attn_W_Q"][hi * 8:(hi + 1) * 8]),
            wk=np.ascontiguousarray(inp["L0_attn_W_K"][hi * 8:(hi + 1) * 8]),
            wv=np.ascontiguousarray(inp["L0_attn_W_V"][hi * 8:(hi + 1) * 8]),
            tril=cst["tril"], negtril=cst["negtril"], identbf=cst["identbf"],
        ))
    res1 = _launch(nc1, in_maps)
    zT_b = [np.concatenate([res1[2 * b]["zT"], res1[2 * b + 1]["zT"]],
                           axis=0) for b in range(4)]

    nc2 = _get("post0", build_post0)
    in_maps = []
    for core in range(8):
        b, hi = core // 2, core % 2
        ts = slice(hi * TOK, (hi + 1) * TOK)
        in_maps.append(dict(
            x_sh=np.ascontiguousarray(x[b][ts]),
            zTm=np.ascontiguousarray(zT_b[b][:, ts]),
            xnTm=np.ascontiguousarray(res1[2 * b]["xnT_d"][:, ts]),
            wo=np.ascontiguousarray(inp["L0_attn_W_O"]),
            fw=np.ascontiguousarray(inp["L0_ffn_weight"]),
            fb=np.ascontiguousarray(inp["L0_ffn_bias"].reshape(1, D)),
            ones1=cst["ones1"], identbf=cst["identbf"],
        ))
    res2 = _launch(nc2, in_maps)
    xn1T_b = [np.concatenate([res2[2 * b]["xn1T"], res2[2 * b + 1]["xn1T"]],
                             axis=1) for b in range(4)]

    nc3 = _get("mamba", build_mamba)
    W = inp["L1_attn_in_proj_w"]
    sel = np.zeros((HL, HL * P_), np.float32)
    for h in range(HL):
        sel[h, h * P_:(h + 1) * P_] = 1.0
    ones4k = np.ones((1, SEQ), np.float32)
    in_maps = []
    for core in range(8):
        b, hi = core // 2, core % 2
        hs = slice(hi * 2048, (hi + 1) * 2048)
        inw = np.concatenate([
            W[:, 4096 + hi * 2048: 4096 + (hi + 1) * 2048],
            W[:, 8192:8320],
            W[:, 8320:8448],
            W[:, 8448 + hi * HL: 8448 + (hi + 1) * HL],
            W[:, hi * 2048:(hi + 1) * 2048],
        ], axis=1)
        chrows = np.r_[hi * 2048:(hi + 1) * 2048, 4096:4224, 4224:4352]
        in_maps.append(dict(
            xn1Tb=np.ascontiguousarray(xn1T_b[b]),
            inw=np.ascontiguousarray(inw),
            convw=np.ascontiguousarray(inp["L1_attn_conv_w"][chrows]),
            convb=np.ascontiguousarray(
                inp["L1_attn_conv_b"][chrows].reshape(2304, 1)),
            dtb=np.ascontiguousarray(
                inp["L1_attn_dt_bias"][hi * HL:(hi + 1) * HL].reshape(HL, 1)),
            alog=np.ascontiguousarray(
                inp["L1_attn_A_log"][hi * HL:(hi + 1) * HL].reshape(HL, 1)),
            dpar=np.ascontiguousarray(
                inp["L1_attn_D_param"][hi * HL:(hi + 1) * HL].reshape(HL, 1)),
            sel=sel, ones1=cst["ones1"], ones4k_d=ones4k, identbf=cst["identbf"],
        ))
    res3 = _launch(nc3, in_maps)
    hT_b = [np.concatenate([res3[2 * b]["hT_sh"], res3[2 * b + 1]["hT_sh"]],
                           axis=0) for b in range(4)]

    nc4 = _get("post1", build_post1)
    in_maps = []
    for core in range(8):
        b, hi = core // 2, core % 2
        ts = slice(hi * TOK, (hi + 1) * TOK)
        in_maps.append(dict(
            x1_sh=np.ascontiguousarray(res2[core]["x1_sh"]),
            hTm=np.ascontiguousarray(hT_b[b][:, ts]),
            opw=np.ascontiguousarray(inp["L1_attn_out_proj_w"]),
            nw=np.ascontiguousarray(
                inp["L1_attn_norm_w"].reshape(32, 128).T),
            gwT=np.ascontiguousarray(inp["L1_ffn_gate_w"].T),
            uwT=np.ascontiguousarray(inp["L1_ffn_up_w"].T),
            dwT=np.ascontiguousarray(inp["L1_ffn_down_w"].T),
            gs=np.ascontiguousarray(inp["L1_ffn_gate_s"].reshape(NIC, 128).T),
            us=np.ascontiguousarray(inp["L1_ffn_up_s"].reshape(NIC, 128).T),
            dsr=np.ascontiguousarray(inp["L1_ffn_down_s"].reshape(1, D)),
            xn1Tm=np.ascontiguousarray(res2[core]["xn1T"]),
            ones1=cst["ones1"],
        ))
    res4 = _launch(nc4, in_maps)
    out = np.stack([np.concatenate([res4[2 * b]["out_sh"],
                                    res4[2 * b + 1]["out_sh"]], axis=0)
                    for b in range(4)])
    return np.ascontiguousarray(out.astype(np.float32))


def _launch(nc, in_maps):
    res = run_bass_kernel_spmd(nc, in_maps, core_ids=list(range(8)))
    return res.results
